# version 07
# baseline (speedup 1.0000x reference)
import math
from contextlib import ExitStack

import numpy as np

import concourse.bass as bass
import concourse.bacc as bacc
import concourse.mybir as mybir
from concourse.bass_utils import run_bass_kernel_spmd

F32 = mybir.dt.float32
BF16 = mybir.dt.bfloat16
I32 = mybir.dt.int32
AF = mybir.ActivationFunctionType
ALU = mybir.AluOpType
ts = bass.ts

NCORE = 8
L = 8192
D = 2048
SEG = L // NCORE
HALO = 512
EXT = SEG + HALO
TT = 512
NTE = EXT // TT
NTO = SEG // TT
DCH = D // 128

H = 16
DR = 64
QL = 1536
KL = 512
QLC = QL // 128
KLC = KL // 128
SC192 = 1.0 / math.sqrt(192.0)
SC128 = 1.0 / math.sqrt(128.0)
EPS_LN = 1e-5
EPS_RMS = 1e-6

GNQ, GNKV, GHD = 16, 4, 128
WINDOW = 512

FH = 8192
EPS_FFN = float(np.finfo(np.float32).eps)

BUILD_UPTO = 99


def _yarn_tables(positions):
    pos2 = np.arange(0, DR, 2, dtype=np.float64)
    inv_freq = 1.0 / (10000.0 ** (pos2 / DR))
    wavelen = 2.0 * np.pi / inv_freq
    ramp = np.clip((8192.0 / wavelen - 1.0) / 3.0, 0.0, 1.0)
    inv_freq = inv_freq * (1.0 - ramp) + inv_freq / 16.0 * ramp
    freqs = np.outer(inv_freq, positions.astype(np.float64))
    attn = 0.1 * math.log(16.0) + 1.0
    cos = np.cos(freqs) * attn
    sin = np.sin(freqs) * attn
    return (np.concatenate([cos, cos], 0).astype(np.float32),
            np.concatenate([-sin, sin], 0).astype(np.float32))


def _gqa_tables(positions):
    half = GHD // 2
    inv_freq = 1.0 / (10000.0 ** (2.0 * np.arange(half, dtype=np.float64) / GHD))
    freqs = np.outer(inv_freq, positions.astype(np.float64))
    return (np.concatenate([np.cos(freqs), np.cos(freqs)], 0).astype(np.float32),
            np.concatenate([-np.sin(freqs), np.sin(freqs)], 0).astype(np.float32))


def _swap_mat(block):
    s = np.zeros((128, 128), np.float32)
    hb = block // 2
    for b in range(128 // block):
        for m in range(block):
            s[b * block + (m + hb) % block, b * block + m] = 1.0
    return s


def _mla_diag_masks():
    k = np.arange(128)[:, None]
    q = np.arange(512)[None, :]
    return np.stack([(q >= k + 128 * j).astype(np.float32) for j in range(4)])


def _gqa_masks(core):
    base = SEG * core - HALO
    out = np.zeros((2, 2, 4, 128, 512), np.float32)
    k = np.arange(128)[:, None]
    q = np.arange(512)[None, :]
    for qt in range(2):
        qpos = base + HALO + 512 * qt + q
        for kb in range(2):
            for sub in range(4):
                kpos = base + 512 * (qt + kb) + 128 * sub + k
                ok = (qpos - kpos >= 0) & (qpos - kpos <= WINDOW) & (kpos >= 0)
                out[qt, kb, sub] = ok.astype(np.float32)
    return out


def _chunked(vec, nch):
    return np.ascontiguousarray(np.asarray(vec).reshape(nch, 128).T).astype(np.float32)


def make_inputs(core, A):
    x = np.asarray(A["L0_attn_a_x"])[0]
    xT = np.ascontiguousarray(x.T).astype(np.float32)
    lo = SEG * core - HALO
    if lo < 0:
        xe = np.concatenate([np.zeros((D, HALO), np.float32),
                             xT[:, :SEG * (core + 1)]], axis=1)
    else:
        xe = np.ascontiguousarray(xT[:, lo:SEG * (core + 1)])

    w_qb = np.asarray(A["L0_attn_a_w_qb"])
    nope_cols = np.concatenate([np.arange(h * 192, h * 192 + 128) for h in range(H)])
    rope_cols = np.concatenate([np.arange(h * 192 + 128, (h + 1) * 192) for h in range(H)])
    w_qb_perm = np.ascontiguousarray(
        np.concatenate([w_qb[:, nope_cols], w_qb[:, rope_cols]], axis=1))

    w_kvb = np.asarray(A["L0_attn_a_w_kvb"])
    kcols = np.concatenate([np.arange(h * 256, h * 256 + 128) for h in range(H)])
    vcols = np.concatenate([np.arange(h * 256 + 128, (h + 1) * 256) for h in range(H)])
    w_kvb_perm = np.ascontiguousarray(
        np.concatenate([w_kvb[:, kcols], w_kvb[:, vcols]], axis=1))

    perm = np.concatenate([np.concatenate([np.arange(h * GHD, h * GHD + GHD, 2),
                                           np.arange(h * GHD + 1, h * GHD + GHD, 2)])
                           for h in range(GNQ)])
    permk = perm[:GNKV * GHD]
    Wq = np.asarray(A["L0_attn_b_Wq"])[:, perm]
    bq = np.asarray(A["L0_attn_b_bq"])[perm]
    Wk = np.asarray(A["L0_attn_b_Wk"])[:, permk]
    bk = np.asarray(A["L0_attn_b_bk"])[permk]

    pos_ext = np.arange(lo, SEG * (core + 1))
    pos_ext = np.where(pos_ext < 0, 0, pos_ext)
    ycos, ysin = _yarn_tables(np.arange(L))
    ycos_e, ysin_e = _yarn_tables(pos_ext)
    gcos_e, gsin_e = _gqa_tables(pos_ext)

    meta = np.zeros((1, 16), np.int32)
    meta[0, 0] = 2 * core + 2
    for qt in range(3):
        meta[0, 1 + qt] = max(1, 2 * core + qt)
    meta[0, 4] = max(1, 2 * core) + max(1, 2 * core + 1) + (2 * core + 2)
    meta[0, 5] = 0
    meta[0, 6] = max(1, 2 * core)
    meta[0, 7] = max(1, 2 * core) + max(1, 2 * core + 1)

    return {
        "x_ext": xe,
        "x_full": xT,
        "meta": meta,
        "w_qa": np.ascontiguousarray(A["L0_attn_a_w_qa"]).astype(np.float32),
        "w_qb_p": w_qb_perm.astype(np.float32),
        "w_kva": np.ascontiguousarray(A["L0_attn_a_w_kva"]).astype(np.float32),
        "w_kvb_p": w_kvb_perm.astype(np.float32),
        "w_o": np.ascontiguousarray(A["L0_attn_a_w_o"]).astype(np.float32),
        "q_norm_w": _chunked(A["L0_attn_a_q_norm_w"], QLC),
        "kv_norm_w": _chunked(A["L0_attn_a_kv_norm_w"], KLC),
        "Wq_p": np.ascontiguousarray(Wq).astype(np.float32),
        "bq_p": _chunked(bq, GNQ),
        "Wk_p": np.ascontiguousarray(Wk).astype(np.float32),
        "bk_p": _chunked(bk, GNKV),
        "Wv": np.ascontiguousarray(A["L0_attn_b_Wv"]).astype(np.float32),
        "bv_row": np.asarray(A["L0_attn_b_bv"]).reshape(1, 512).astype(np.float32),
        "Wo_b": np.ascontiguousarray(A["L0_attn_b_Wo"]).astype(np.float32),
        "bo_p": _chunked(A["L0_attn_b_bo"], DCH),
        "wg": np.ascontiguousarray(np.asarray(A["L0_ffn_w_gate_up"])).astype(np.float32),
        "wp": np.ascontiguousarray(np.asarray(A["L0_ffn_w_proj"])).astype(np.float32),
        "ycos": ycos, "ysin": ysin,
        "ycos_e": np.ascontiguousarray(np.concatenate([ycos_e, ycos_e], 0)),
        "ysin_e": np.ascontiguousarray(np.concatenate([ysin_e, ysin_e], 0)),
        "gcos_e": gcos_e, "gsin_e": gsin_e,
        "swap64": _swap_mat(64), "swap128": _swap_mat(128),
        "mla_masks": _mla_diag_masks().reshape(4 * 128, 512).astype(np.float32),
        "gqa_masks": _gqa_masks(core).reshape(16 * 128, 512).astype(np.float32),
    }


def build_program(upto=99):
    nc = bacc.Bacc("TRN2", target_bir_lowering=False)
    nc.detect_race_conditions = False

    def dt_in(name, shape, dt=F32):
        return nc.dram_tensor(name, shape, dt, kind="ExternalInput").ap()

    def dt_scr(name, shape, dt=BF16):
        return nc.dram_tensor(name, shape, dt).ap()

    G = {}
    for nm, sh, dt in [
        ("x_ext", [D, EXT], F32), ("x_full", [D, L], F32), ("meta", [1, 16], I32),
        ("w_qa", [D, QL], F32), ("w_qb_p", [QL, 3072], F32),
        ("w_kva", [D, 576], F32), ("w_kvb_p", [KL, 4096], F32), ("w_o", [D, D], F32),
        ("q_norm_w", [128, QLC], F32), ("kv_norm_w", [128, KLC], F32),
        ("Wq_p", [D, D], F32), ("bq_p", [128, GNQ], F32),
        ("Wk_p", [D, 512], F32), ("bk_p", [128, GNKV], F32),
        ("Wv", [D, 512], F32), ("bv_row", [1, 512], F32),
        ("Wo_b", [D, D], F32), ("bo_p", [128, DCH], F32),
        ("wg", [2 * FH, D], F32), ("wp", [D, FH], F32),
        ("ycos", [64, L], F32), ("ysin", [64, L], F32),
        ("ycos_e", [128, EXT], F32), ("ysin_e", [128, EXT], F32),
        ("gcos_e", [128, EXT], F32), ("gsin_e", [128, EXT], F32),
        ("swap64", [128, 128], F32), ("swap128", [128, 128], F32),
        ("mla_masks", [4 * 128, 512], F32), ("gqa_masks", [16 * 128, 512], F32),
    ]:
        G[nm] = dt_in(nm, sh, dt)

    yout = nc.dram_tensor("yout", [D, SEG], F32, kind="ExternalOutput").ap()

    latscr = dt_scr("latscr", [5, 128, L])
    kscr = dt_scr("kscr", [H, 128, L])
    vscr = dt_scr("vscr", [L // 128, 128, H * 128])
    qscr = dt_scr("qscr", [24, 128, EXT])
    wgq = dt_scr("wgq", [2 * FH, D])
    hscr2 = dt_scr("hscr2", [FH // 128, 128, TT])
    wpq = dt_scr("wpq", [D, FH])

    es = ExitStack()
    with es:
        SB = lambda name, shape, dt=BF16: es.enter_context(nc.sbuf_tensor(name, shape, dt))
        SEM = lambda name: es.enter_context(nc.semaphore(name))

        sb_x = SB("sb_x", [128, DCH * EXT], F32)
        sb_meta = SB("sb_meta", [1, 16], I32)
        sb_ones = SB("sb_ones", [128, 1], BF16)
        sb_onesr = SB("sb_onesr", [1, 128], BF16)
        sb_onesr32 = SB("sb_onesr32", [1, 128], F32)
        sb_zrow = SB("sb_zrow", [1, 128], BF16)
        sb_qnw = SB("sb_qnw", [128, QLC], F32)
        sb_bqp = SB("sb_bqp", [128, GNQ], F32)
        sb_bkp = SB("sb_bkp", [128, GNKV], F32)
        sb_bop = SB("sb_bop", [128, DCH], F32)
        sb_kvnw = SB("sb_kvnw", [128, KLC], F32)
        sb_swap64 = SB("sb_swap64", [128, 128], BF16)
        sb_swap128 = SB("sb_swap128", [128, 128], BF16)
        sb_mrow = SB("sb_mrow", [1, 64], F32)
        sb_mtil = SB("sb_mtil", [128, H + GNQ], F32)
        arenaA = SB("arenaA", [128, 27 * 1024 - 256], BF16)
        arenaB = SB("arenaB", [128, 22016], BF16)
        rowsb2 = SB("rowsb2", [1, 512], BF16)
        rows2q = SB("rows2q", [128, 1024], F32)
        rows = SB("rows", [1, 1536], F32)
        rowsb = SB("rowsb", [1, 1024], BF16)

        sb_const = SB("sb_const", [1, 8], F32)
        pbt = [es.enter_context(nc.psum_tensor("pb%d" % i, [128, 512], F32))
               for i in range(8)]
        pb = [t[:, :] for t in pbt]

        def pbvr_const(bank, r0, r1):
            return pbt[bank][r0:r1, :]

        class Chain:
            def __init__(self, cs, hs, ss_, js=None):
                self.cs, self.hs, self.ss, self.js = cs, hs, ss_, js
                self.nc_, self.nh, self.ns, self.nj = 0, 0, 0, 0
                self.ops = []

            def add(self, eng, fn, kind="c", n=1, wc=None, wh=None, ws=None,
                    wj=None):
                c0 = self.nc_ if wc is None else wc
                h0 = self.nh if wh is None else wh
                s0 = self.ns if ws is None else ws
                j0 = self.nj if wj is None else wj
                self.ops.append((eng, fn, kind, c0, h0, s0, j0,
                                 kind != "c" and n > 1))
                if kind == "c":
                    self.nc_ += n
                elif kind == "h":
                    self.nh += 16 * n
                elif kind == "j":
                    self.nj += 16 * n
                else:
                    self.ns += 16 * n

            def emit(self):
                for eng in ENGS:
                    sub = [o for o in self.ops if o[0] == eng]
                    if not sub:
                        continue

                    def body(e, sub=sub):
                        for _, fn, kind, c0, h0, s0, j0, selfinc in sub:
                            if c0:
                                e.wait_ge(self.cs, c0)
                            if h0:
                                e.wait_ge(self.hs, h0)
                            if s0:
                                e.wait_ge(self.ss, s0)
                            if j0:
                                e.wait_ge(self.js, j0)
                            last = fn(e)
                            if selfinc:
                                continue
                            if kind == "c":
                                last.then_inc(self.cs, 1)
                            elif kind == "h":
                                last.then_inc(self.hs, 16)
                            elif kind == "j":
                                last.then_inc(self.js, 16)
                            else:
                                last.then_inc(self.ss, 16)
                    getattr(blk, eng)(body)
                for eng in ENGS:
                    def tail(e):
                        if self.nc_:
                            e.wait_ge(self.cs, self.nc_)
                        if self.nh:
                            e.wait_ge(self.hs, self.nh)
                        if self.ns:
                            e.wait_ge(self.ss, self.ns)
                        if self.nj:
                            e.wait_ge(self.js, self.nj)
                    getattr(blk, eng)(tail)

        class LoopChain:
            def __init__(self, cs, dma_sems):
                self.cs = cs
                self.dma = dict(dma_sems)
                self.ops = []
                self.cnt = {"c": 0}
                for k in self.dma:
                    self.cnt[k] = 0

            def add(self, eng, fn, kind="c"):
                self.ops.append((eng, fn, kind, dict(self.cnt)))
                self.cnt[kind] += 1 if kind == "c" else 16

            def emit(self, nidx, base=None, extra_entry=None):
                per = dict(self.cnt)
                for eng in ENGS:
                    sub = [o for o in self.ops if o[0] == eng]
                    if not sub:
                        continue

                    def body(e, sub=sub, per=per):
                        n = nv(e, nidx) if isinstance(nidx, int) else None
                        nn = n if n is not None else nidx
                        bexpr = base(e) if base is not None else None
                        if extra_entry is not None:
                            extra_entry(e, bexpr)
                        with e.Fori(0, nn) as i:
                            ib = (i + bexpr) if bexpr is not None else i
                            for _, fn, kind, snap in sub:
                                if snap["c"]:
                                    e.wait_ge(self.cs, ib * per["c"] + snap["c"])
                                else:
                                    with e.If(ib > 0):
                                        e.wait_ge(self.cs, ib * per["c"])
                                for k, sem in self.dma.items():
                                    if snap[k]:
                                        e.wait_ge(sem, ib * per[k] + snap[k])
                                last = fn(e, i, ib)
                                if kind == "c":
                                    last.then_inc(self.cs, 1)
                                else:
                                    last.then_inc(self.dma[kind], 16)
                        nend = (nn + bexpr) if bexpr is not None else nn
                        if per["c"]:
                            e.wait_ge(self.cs, nend * per["c"])
                        for k, sem in self.dma.items():
                            if per[k]:
                                e.wait_ge(sem, nend * per[k])
                    getattr(blk, eng)(body)

        semnames = ["bar"]
        for grp in "abcdefgnzquwo":
            for i in range(6):
                semnames.append("%s%d" % (grp, i))
        S = {nm: SEM(nm) for nm in semnames}

        CONSTS = [EPS_LN, EPS_RMS, EPS_FFN, 1.0]
        for ci, cv in enumerate(CONSTS):
            nc.const_aps.register(F32, cv, sb_const[0:1, ci:ci + 1])

        blk = es.enter_context(nc.Block("prog"))
        ENGS = ["sync", "gpsimd", "scalar", "vector", "tensor"]
        barn = [0]

        def barrier():
            barn[0] += 1
            tgt = 5 * barn[0]
            for en in ENGS:
                def f(e, tgt=tgt):
                    e.nop().then_inc(S["bar"], 1)
                    e.wait_ge(S["bar"], tgt)
                getattr(blk, en)(f)

        def nv(e, idx=0, lo=1, hi=16):
            return e.value_load(sb_meta[:1, idx:idx + 1], min_val=lo, max_val=hi)

        def chv(ap_, nch):
            return ap_.rearrange("p (c t) -> p c t", c=nch)

        def aA(off, size):
            return arenaA[:, off:off + size]

        def aB(off, size):
            return arenaB[:, off:off + size]

        def xh(p):
            return arenaA[:, ts(p + 1, 8192)]

        def rawv(p):
            return arenaB[:, 2048:7168][:, ts(p, 2560)]

        def absbv(p):
            return arenaB[:, 0:2048][:, ts(p, 1024)]


        def p0():
            def f(e):
                e.dma_start(out=sb_meta[:], in_=G["meta"]).then_inc(S["u0"], 16)
                e.dma_start(out=sb_qnw[:], in_=G["q_norm_w"]).then_inc(S["u0"], 16)
                e.dma_start(out=sb_kvnw[:], in_=G["kv_norm_w"]).then_inc(S["u0"], 16)
                e.dma_start(out=sb_bqp[:], in_=G["bq_p"]).then_inc(S["u3"], 16)
                e.dma_start(out=sb_bkp[:], in_=G["bk_p"]).then_inc(S["u3"], 16)
                e.dma_start(out=sb_bop[:], in_=G["bo_p"]).then_inc(S["u3"], 16)
                e.dma_start(
                    out=sb_x[:].rearrange("p (c t) -> p c t", c=DCH),
                    in_=G["x_ext"].rearrange("(c p) t -> p c t", p=128),
                ).then_inc(S["u0"], 16)
                e.wait_ge(S["u0"], 64)
                e.wait_ge(S["u3"], 48)
            blk.sync(f)

            def f(e):
                e.dma_start(out=sb_swap64[:], in_=G["swap64"]).then_inc(S["u1"], 16)
                e.dma_start(out=sb_swap128[:], in_=G["swap128"]).then_inc(S["u1"], 16)
                for ci, cv in enumerate([EPS_LN, EPS_RMS, EPS_FFN, 1.0]):
                    e.memset(sb_const[0:1, ci:ci + 1], cv)
                e.memset(sb_ones[:], 1.0)
                e.memset(sb_onesr[:], 1.0)
                e.memset(sb_onesr32[:], 1.0)
                e.memset(sb_zrow[:], 0.0)
                e.memset(sb_mrow[:], 0.0).then_inc(S["u2"], 1)
                e.wait_ge(S["u1"], 32)
                e.wait_ge(S["u2"], 1)
            blk.gpsimd(f)

        DC1 = 31

        def p1():
            def ifw(e, cond_expr, sem, val):
                with e.If(cond_expr):
                    e.wait_ge(S[sem], val)

            def f(e):
                e.wait_ge(S["u0"], 64)
                e.wait_ge(S["u3"], 48)
                e.wait_ge(S["u1"], 32)
                e.wait_ge(S["u2"], 1)
                n1 = nv(e)
                with e.Fori(0, n1) as i:
                    p = i % 2
                    with e.If(i > 1):
                        e.wait_ge(S["a0"], DC1 * (i - 1))
                    with e.If(i > 0):
                        e.wait_ge(S["a0"], DC1 * (i - 1) + 9)
                    e.dma_start(
                        out=chv(arenaA[:, 0:8192], DCH),
                        in_=G["x_full"].rearrange("(c p) t -> p c t", p=128)[:, :, ts(i, TT)],
                    ).then_inc(S["a2"], 16)
                    with e.If(i > 0):
                        e.wait_ge(S["a0"], DC1 * i)
                    e.dma_start(out=aB(9728, 1024).bitcast(F32)[0:64, :],
                                in_=G["ycos"][:, ts(i, TT)]).then_inc(S["a3"], 16)
                    e.dma_start(out=aB(10752, 1024).bitcast(F32)[0:64, :],
                                in_=G["ysin"][:, ts(i, TT)]).then_inc(S["a3"], 16)
                    for c in range(5):
                        mw = 128 if c < 4 else 64
                        with e.If(i > 0):
                            e.wait_ge(S["a0"], DC1 * (i - 1) + 11 + 3 * c)
                        e.dma_start(
                            out=arenaB[:, 11776 + c * 2048:11776 + (c + 1) * 2048]
                            .rearrange("p (d m) -> p d m", d=16)[:, :, 0:mw],
                            in_=G["w_kva"].rearrange("(d p) m -> p d m", p=128)
                            [:, :, c * 128:c * 128 + mw],
                        ).then_inc(S["a4"], 16)
                    e.wait_ge(S["a0"], DC1 * i + 8)
                    e.tensor_tensor(
                        chv(xh(p), DCH),
                        chv(arenaA[:, 0:8192], DCH),
                        absbv(p)[:, 0:512]
                        .unsqueeze(1).broadcast_to((128, DCH, 512)),
                        ALU.mult).then_inc(S["a0"], 1)
                    with e.If(i > 0):
                        e.wait_ge(S["a0"], DC1 * (i - 1) + 25)
                    for c in range(4):
                        e.wait_ge(S["a0"], DC1 * i + 12 + 3 * c)
                        e.tensor_tensor(
                            aB(7168 + c * 512, 512),
                            rawv(p)[:, c * 512:(c + 1) * 512],
                            rawv(p)[:, c * 512:(c + 1) * 512],
                            ALU.mult).then_inc(S["a0"], 1)
            blk.gpsimd(f)

            def f(e):
                n1 = nv(e)
                with e.Fori(0, n1) as i:
                    p = i % 2
                    with e.If(i > 1):
                        e.wait_ge(S["a0"], DC1 * (i - 1))
                    e.wait_ge(S["a2"], 16 * i + 16)
                    e.activation(xh(p),
                                 arenaA[:, 0:8192], AF.Square).then_inc(S["a0"], 1)
                    e.wait_ge(S["a0"], DC1 * i + 3)
                    e.activation(rows[:1, 1024:1536],
                                 rows[:1, 512:1024],
                                 AF.Ln, bias=EPS_LN).then_inc(S["a0"], 1)
                    e.wait_ge(S["a0"], DC1 * i + 4)
                    e.activation(rows[:1, 1024:1536],
                                 rows[:1, 1024:1536],
                                 AF.Exp, scale=-0.5).then_inc(S["a0"], 1)
                    e.wait_ge(S["a0"], DC1 * i + 7)
                    e.copy(absbv(p)[:, 0:512], pb[2])
                    e.copy(absbv(p)[:, 512:1024], pb[4]) \
                        .then_inc(S["a0"], 1)
                    for c in range(5):
                        e.wait_ge(S["a0"], DC1 * i + 11 + 3 * c)
                        with e.If(i > 0):
                            e.wait_ge(S["a1"], 80 * i)
                        mw = 128 if c < 4 else 64
                        e.copy(rawv(p)[:, c * 512:(c + 1) * 512][0:mw, :],
                               pb[6 + (c % 2)][0:mw, :]).then_inc(S["a0"], 1)
                    e.wait_ge(S["a0"], DC1 * i + 25)
                    e.activation(rows[:1, 512:1024],
                                 pb[2][0:1, :], AF.Ln, bias=EPS_RMS,
                                 scale=1.0 / KL).then_inc(S["a0"], 1)
                    e.wait_ge(S["a0"], DC1 * i + 26)
                    e.activation(rows[:1, 512:1024],
                                 rows[:1, 512:1024],
                                 AF.Exp, scale=-0.5).then_inc(S["a0"], 1)
                    e.wait_ge(S["a0"], DC1 * i + 27)
                    with e.If(i > 0):
                        e.wait_ge(S["a0"], DC1 * (i - 1) + 29)
                    e.copy(rowsb2[:1, 0:512],
                           rows[:1, 512:1024]).then_inc(S["a0"], 1)
            blk.scalar(f)

            def f(e):
                n1 = nv(e)
                with e.Fori(0, n1) as i:
                    p = i % 2
                    with e.If(i > 1):
                        e.wait_ge(S["a0"], DC1 * (i - 1))
                    e.wait_ge(S["a0"], DC1 * i + 2)
                    with e.If(i > 0):
                        e.wait_ge(S["a0"], DC1 * (i - 1) + 28)
                    e.tensor_scalar_mul(rows[:1, 0:512],
                                        pb[0][0:1, :], 1.0 / D)
                    e.tensor_tensor(rows[:1, 512:1024],
                                    rows[:1, 0:512],
                                    rows[:1, 0:512], ALU.mult)
                    e.scalar_tensor_tensor(rows[:1, 512:1024],
                                           pb[0][32:33, :], 1.0 / D,
                                           rows[:1, 512:1024],
                                           ALU.mult, ALU.subtract).then_inc(S["a0"], 1)
                    e.wait_ge(S["a0"], DC1 * i + 5)
                    e.tensor_copy(rowsb[:1, 0:512],
                                  rows[:1, 1024:1536])
                    e.tensor_tensor(rows[:1, 0:512],
                                    rows[:1, 0:512],
                                    rows[:1, 1024:1536], ALU.mult)
                    e.tensor_scalar_mul(rowsb[:1, 512:1024],
                                        rows[:1, 0:512],
                                        -1.0).then_inc(S["a0"], 1)
                    e.wait_ge(S["a0"], DC1 * i + 9)
                    e.tensor_tensor(
                        chv(xh(p), DCH),
                        chv(xh(p), DCH),
                        absbv(p)[:, 512:1024]
                        .unsqueeze(1).broadcast_to((128, DCH, 512)),
                        ALU.add).then_inc(S["a0"], 1)
                    e.wait_ge(S["a0"], DC1 * i + 29)
                    with e.If(i > 0):
                        e.wait_ge(S["a1"], 80 * i)
                    for c in range(4):
                        e.scalar_tensor_tensor(
                            aA(24576 + c * 512, 512),
                            rawv(p)[:, c * 512:(c + 1) * 512],
                            sb_kvnw[:, c:c + 1],
                            pb[2][:, :],
                            ALU.mult, ALU.mult)
                    e.nop().then_inc(S["a0"], 1)
                    e.wait_ge(S["a3"], 32 * i + 32)
                    e.tensor_tensor(aA(24576 + 4 * 512, 512)[0:64, :],
                                    rawv(p)[:, 4 * 512:5 * 512][0:64, :],
                                    aB(9728, 1024).bitcast(F32)[0:64, 0:512],
                                    ALU.mult)
                    e.tensor_tensor(aB(9216, 512)[0:64, :],
                                    pb[4][0:64, :],
                                    aB(10752, 1024).bitcast(F32)[0:64, 0:512],
                                    ALU.mult)
                    e.tensor_tensor(aA(24576 + 4 * 512, 512)[0:64, :],
                                    aA(24576 + 4 * 512, 512)[0:64, :],
                                    aB(9216, 512)[0:64, :], ALU.add).then_inc(S["a0"], 1)
            blk.vector(f)

            def f(e):
                e.wait_ge(S["u1"], 32)
                e.wait_ge(S["u2"], 1)
                n1 = nv(e)
                with e.Fori(0, n1) as i:
                    p = i % 2
                    with e.If(i > 1):
                        e.wait_ge(S["a0"], DC1 * (i - 1))
                    e.wait_ge(S["a2"], 16 * i + 16)
                    e.wait_ge(S["a0"], DC1 * i + 1)
                    with e.If(i > 0):
                        e.wait_ge(S["a0"], DC1 * (i - 1) + 3)
                    for d in range(DCH):
                        e.matmul(pb[0][0:1, :], sb_ones[:],
                                 arenaA[:, d * 512:(d + 1) * 512],
                                 start=(d == 0), stop=(d == DCH - 1))
                    mm = None
                    for d in range(DCH):
                        mm = e.matmul(pb[0][32:33, :], sb_ones[:],
                                      xh(p)[:, d * 512:(d + 1) * 512],
                                      start=(d == 0), stop=(d == DCH - 1),
                                      tile_position=(0, 32))
                    mm.then_inc(S["a0"], 1)
                    e.wait_ge(S["a0"], DC1 * i + 6)
                    with e.If(i > 0):
                        e.wait_ge(S["a0"], DC1 * i)
                    e.matmul(pb[2][:, :], sb_onesr[:],
                             rowsb[:1, 0:512], start=True, stop=True)
                    e.matmul(pb[4][:, :], sb_onesr[:],
                             rowsb[:1, 512:1024],
                             start=True, stop=True).then_inc(S["a0"], 1)
                    e.wait_ge(S["a0"], DC1 * i + 10)
                    for c in range(5):
                        e.wait_ge(S["a4"], 80 * i + 16 * (c + 1))
                        if c < 2:
                            with e.If(i > 0):
                                e.wait_ge(S["a0"], DC1 * (i - 1) + 12 + 3 * 3 + 3 * c)
                        else:
                            e.wait_ge(S["a0"], DC1 * i + 12 + 3 * (c - 2))
                        mw = 128 if c < 4 else 64
                        mm = None
                        for d in range(DCH):
                            mm = e.matmul(
                                pb[6 + (c % 2)][0:mw, :],
                                arenaB[:, 11776 + c * 2048 + d * 128:
                                       11776 + c * 2048 + d * 128 + mw],
                                xh(p)[:, d * 512:(d + 1) * 512],
                                start=(d == 0), stop=(d == DCH - 1))
                        mm.then_inc(S["a0"], 1)
                    e.wait_ge(S["a0"], DC1 * i + 24)
                    mm = None
                    for c in range(4):
                        mm = e.matmul(pb[2][0:1, :], sb_ones[:],
                                      aB(7168 + c * 512, 512),
                                      start=(c == 0), stop=(c == 3))
                    e.matmul(pb[4][0:64, :], sb_swap64[0:64, 0:64],
                             rawv(p)[:, 4 * 512:5 * 512][0:64, :],
                             start=True, stop=True).then_inc(S["a0"], 1)
                    e.wait_ge(S["a0"], DC1 * i + 28)
                    e.matmul(pb[2][:, :], sb_onesr[:],
                             rowsb2[:1, 0:512],
                             start=True, stop=True).then_inc(S["a0"], 1)
            blk.tensor(f)

            def f(e):
                n1 = nv(e)
                with e.Fori(0, n1) as i:
                    p = i % 2
                    e.wait_ge(S["a0"], DC1 * i + 31)
                    for c in range(4):
                        e.dma_start(
                            out=latscr[c][:, ts(i, TT)],
                            in_=aA(24576 + c * 512, 512),
                        ).then_inc(S["a1"], 16)
                    e.dma_start(
                        out=latscr[4][0:64, ts(i, TT)],
                        in_=aA(24576 + 4 * 512, 512)[0:64, :],
                    ).then_inc(S["a1"], 16)
                n1b = nv(e)
                e.wait_ge(S["a1"], n1b * 80)
            blk.sync(f)


        def p2():
            ch = Chain(S["b0"], S["b1"], S["b2"])
            XB = aA(0, 8192)
            X2 = aA(8192, 8192)
            QRAW = aB(0, 6144)
            QLN = aB(6144, 6144)
            WTI = aB(12288, 2048)
            QOUT = aB(14336, 512)
            ROPEB = aB(14848, 512)
            Q2 = aA(16384, 512)
            COSE = aB(15360, 1024).bitcast(F32)
            SINE = aB(16384, 1024).bitcast(F32)
            T1R = aA(16896, 512)
            for tt in range(NTE):
                ch.add("gpsimd", lambda e, tt=tt: e.tensor_copy(
                    chv(XB, DCH),
                    sb_x[:].rearrange("p (c t) -> p c t", c=DCH)[:, :, tt * TT:(tt + 1) * TT]))
                ch.add("scalar", lambda e: e.activation(X2, XB, AF.Square))
                def stats(e):
                    for d in range(DCH):
                        e.matmul(pb[0][0:1, :], sb_ones[:], XB[:, d * 512:(d + 1) * 512],
                                 start=(d == 0), stop=(d == DCH - 1))
                    mm = None
                    for d in range(DCH):
                        mm = e.matmul(pb[0][32:33, :], sb_ones[:],
                                      X2[:, d * 512:(d + 1) * 512],
                                      start=(d == 0), stop=(d == DCH - 1),
                                      tile_position=(0, 32))
                    return mm
                ch.add("tensor", stats)
                def mv(e):
                    e.tensor_scalar_mul(rows[:1, 0:512], pb[0][0:1, :], 1.0 / D)
                    e.tensor_tensor(rows[:1, 512:1024], rows[:1, 0:512],
                                    rows[:1, 0:512], ALU.mult)
                    return e.scalar_tensor_tensor(rows[:1, 512:1024], pb[0][32:33, :],
                                                  1.0 / D, rows[:1, 512:1024],
                                                  ALU.mult, ALU.subtract)
                ch.add("vector", mv)
                ch.add("scalar", lambda e: e.activation(rows[:1, 1024:1536],
                                                        rows[:1, 512:1024],
                                                        AF.Ln, bias=EPS_LN))
                ch.add("scalar", lambda e: e.activation(rows[:1, 1024:1536],
                                                        rows[:1, 1024:1536],
                                                        AF.Exp, scale=-0.5))
                def ab_(e):
                    e.tensor_tensor(rows[:1, 0:512], rows[:1, 0:512],
                                    rows[:1, 1024:1536], ALU.mult)
                    return e.tensor_scalar_mul(rows[:1, 0:512], rows[:1, 0:512], -1.0)
                ch.add("vector", ab_)
                def bc(e):
                    e.matmul(pb[2][:, :], sb_onesr32[:], rows[:1, 1024:1536],
                             start=True, stop=True)
                    return e.matmul(pb[3][:, :], sb_onesr32[:], rows[:1, 0:512],
                                    start=True, stop=True)
                ch.add("tensor", bc)
                def cp(e):
                    e.copy(rows2q[:, 0:512], pb[2])
                    return e.copy(rows2q[:, 512:1024], pb[3])
                ch.add("scalar", cp)
                ch.add("gpsimd", lambda e: e.tensor_tensor(
                    chv(X2, DCH), chv(XB, DCH),
                    rows2q[:, 0:512].unsqueeze(1).broadcast_to((128, DCH, 512)),
                    ALU.mult))
                ch.add("vector", lambda e: e.tensor_tensor(
                    chv(X2, DCH), chv(X2, DCH),
                    rows2q[:, 512:1024].unsqueeze(1).broadcast_to((128, DCH, 512)),
                    ALU.add))
                for c in range(QLC):
                    ch.add("gpsimd", lambda e, c=c: e.dma_start(
                        out=WTI[:].rearrange("p (d m) -> p d m", d=16),
                        in_=G["w_qa"].rearrange("(d p) m -> p d m", p=128)
                        [:, :, c * 128:(c + 1) * 128]), kind="s")
                    def qmm(e, c=c):
                        mm = None
                        for d in range(DCH):
                            mm = e.matmul(pb[6][:, :],
                                          WTI[:, d * 128:(d + 1) * 128],
                                          X2[:, d * 512:(d + 1) * 512],
                                          start=(d == 0), stop=(d == DCH - 1))
                        return mm
                    ch.add("tensor", qmm)
                    ch.add("scalar", lambda e, c=c: e.copy(
                        QRAW[:, c * 512:(c + 1) * 512], pb[6]))
                    ch.add("gpsimd", lambda e, c=c: e.tensor_tensor(
                        Q2, QRAW[:, c * 512:(c + 1) * 512],
                        QRAW[:, c * 512:(c + 1) * 512], ALU.mult))
                    ch.add("tensor", lambda e, c=c: e.matmul(
                        pb[5][0:1, :], sb_ones[:], Q2,
                        start=(c == 0), stop=(c == QLC - 1)))
                ch.add("scalar", lambda e: e.activation(
                    rows[:1, 512:1024], pb[5][0:1, :], AF.Ln,
                    bias=EPS_RMS, scale=1.0 / QL))
                ch.add("scalar", lambda e: e.activation(
                    rows[:1, 512:1024], rows[:1, 512:1024], AF.Exp, scale=-0.5))
                ch.add("scalar", lambda e: e.copy(rowsb2[:1, 0:512], rows[:1, 512:1024]))
                ch.add("tensor", lambda e: e.matmul(
                    pb[4][:, :], sb_onesr[:], rowsb2[:1, 0:512], start=True, stop=True))
                def qln(e):
                    last = None
                    for c in range(QLC):
                        last = e.scalar_tensor_tensor(
                            QLN[:, c * 512:(c + 1) * 512],
                            QRAW[:, c * 512:(c + 1) * 512],
                            sb_qnw[:, c:c + 1], pb[4][:, :], ALU.mult, ALU.mult)
                    return last
                ch.add("vector", qln)
                ch.add("gpsimd", lambda e, tt=tt: e.dma_start(
                    out=COSE[:, 0:512], in_=G["ycos_e"][:, tt * TT:(tt + 1) * TT]), kind="s")
                ch.add("gpsimd", lambda e, tt=tt: e.dma_start(
                    out=SINE[:, 0:512], in_=G["ysin_e"][:, tt * TT:(tt + 1) * TT]), kind="s")
                for r in range(8):
                    ms = [2 * r, 2 * r + 1, 16 + r]
                    for m in ms:
                        ch.add("gpsimd", lambda e, m=m: e.dma_start(
                            out=WTI[:, 0:1536].rearrange("p (d m2) -> p d m2", d=12),
                            in_=G["w_qb_p"].rearrange("(d p) m2 -> p d m2", p=128)
                            [:, :, m * 128:(m + 1) * 128]), kind="s")
                        def pmm(e, m=m):
                            mm = None
                            for c in range(QLC):
                                mm = e.matmul(pb[6][:, :],
                                              WTI[:, c * 128:(c + 1) * 128],
                                              QLN[:, c * 512:(c + 1) * 512],
                                              start=(c == 0), stop=(c == QLC - 1))
                            return mm
                        ch.add("tensor", pmm)
                        if m < 16:
                            ch.add("scalar", lambda e: e.copy(QOUT, pb[6]))
                            ch.add("gpsimd", lambda e: e.tensor_tensor(
                                Q2, QOUT, QOUT, ALU.mult))
                            ch.add("tensor", lambda e, m=m: e.matmul(
                                pbvr_const(5, 32 * (m % 2), 32 * (m % 2) + 1),
                                sb_ones[:], Q2, start=True, stop=True,
                                tile_position=(0, 32 * (m % 2))))
                        else:
                            ch.add("scalar", lambda e: e.copy(ROPEB, pb[6]))
                            ch.add("tensor", lambda e: e.matmul(
                                pb[1][:, :], sb_swap64[:, :], ROPEB,
                                start=True, stop=True))
                            def comb(e):
                                e.tensor_tensor(T1R, ROPEB, COSE[:, 0:512], ALU.mult)
                                e.tensor_tensor(QOUT, pb[1][:, :], SINE[:, 0:512],
                                                ALU.mult)
                                return e.tensor_tensor(QOUT, QOUT, T1R, ALU.add)
                            ch.add("vector", comb)
                            ch.add("gpsimd", lambda e: e.tensor_tensor(
                                Q2, QOUT, QOUT, ALU.mult))
                            def ssrope(e):
                                e.matmul(pbvr_const(4, 0, 1),
                                         sb_ones[0:64, :], Q2[0:64, :],
                                         start=True, stop=True,
                                         tile_position=(0, 0))
                                return e.matmul(pbvr_const(4, 32, 33),
                                                sb_ones[64:128, :], Q2[64:128, :],
                                                start=True, stop=True,
                                                tile_position=(64, 32))
                            ch.add("tensor", ssrope)
                        ch.add("sync", lambda e, m=m, tt=tt: e.dma_start(
                            out=qscr[m][:, tt * TT:(tt + 1) * TT], in_=QOUT), kind="h")
                    for k in range(2):
                        def gsum(e, k=k):
                            e.tensor_copy(rows[:1, 0:512],
                                          pbt[5][32 * k:32 * k + 1, :])
                            e.tensor_tensor(rows[:1, 0:512], rows[:1, 0:512],
                                            pbt[4][32 * k:32 * k + 1, :],
                                            ALU.add)
                            return e.tensor_reduce(rows[:1, 1024 + k:1025 + k],
                                                   rows[:1, 0:512],
                                                   mybir.AxisListType.X, ALU.max)
                        ch.add("vector", gsum)
                    def gmax(e, r=r):
                        last = None
                        for k in range(2):
                            hh = 2 * r + k
                            last = e.tensor_tensor(sb_mrow[:1, hh:hh + 1],
                                                   sb_mrow[:1, hh:hh + 1],
                                                   rows[:1, 1024 + k:1025 + k],
                                                   ALU.max)
                        return last
                    ch.add("vector", gmax)
            ch.emit()


        def p5():
            lc5 = LoopChain(S["d0"], {"w": S["d1"], "l": S["d2"], "g": S["d3"]})
            LAT = aB(0, 2560)
            KWT = aB(2560, 8192)
            VWT = aB(10752, 8192)
            K2 = aA(0, 2048)
            KOUT = aA(2048, 2048)
            VOUT = aA(4096, 2048)
            lc5.add("sync", lambda e, i, ib: e.dma_start(
                out=LAT[:, 0:2048].rearrange("p (c t) -> p c t", c=4),
                in_=latscr[0:4, :, ts(i, TT)].rearrange("c p t -> p c t")), kind="l")
            lc5.add("sync", lambda e, i, ib: e.dma_start(
                out=LAT[0:64, 2048:2560],
                in_=latscr[4][0:64, ts(i, TT)]), kind="l")
            lc5.add("gpsimd", lambda e, i, ib: e.dma_start(
                out=KWT[:].rearrange("p (c m) -> p c m", c=4),
                in_=G["w_kvb_p"].rearrange("(c p) m -> p c m", p=128)[:, :, 0:2048]),
                kind="g")
            lc5.add("gpsimd", lambda e, i, ib: e.dma_start(
                out=VWT[:].rearrange("p (c m) -> p c m", c=4),
                in_=G["w_kvb_p"].rearrange("(c p) m -> p c m", p=128)[:, :, 2048:4096]),
                kind="g")
            for hg in range(4):
                def kM(e, i, ib, hg=hg):
                    mm = None
                    for q in range(4):
                        for lc in range(4):
                            mm = e.matmul(
                                pb[4 + q][:, :],
                                KWT[:, lc * 2048 + (4 * hg + q) * 128:
                                    lc * 2048 + (4 * hg + q) * 128 + 128],
                                LAT[:, lc * 512:(lc + 1) * 512],
                                start=(lc == 0), stop=(lc == 3))
                    return mm
                lc5.add("tensor", kM)
                def kE(e, i, ib):
                    last = None
                    for q in range(4):
                        last = e.copy(KOUT[:, q * 512:(q + 1) * 512], pb[4 + q])
                    return last
                lc5.add("scalar", kE)
                lc5.add("gpsimd", lambda e, i, ib: e.tensor_tensor(
                    K2, KOUT, KOUT, ALU.mult))
                def kS(e, i, ib, hg=hg):
                    mm = None
                    for q in range(4):
                        mm = e.matmul(pb[0][0:1, :], sb_ones[:],
                                      K2[:, q * 512:(q + 1) * 512],
                                      start=(hg == 0 and q == 0), stop=False)
                    return mm
                lc5.add("tensor", kS)
                lc5.add("sync", lambda e, i, ib, hg=hg: e.dma_start(
                    out=kscr[4 * hg:4 * hg + 4, :, ts(i, TT)].rearrange("c p t -> p c t"),
                    in_=KOUT[:].rearrange("p (c t) -> p c t", c=4)), kind="w")
            lc5.add("gpsimd", lambda e, i, ib: e.tensor_tensor(
                K2[0:64, 0:512], LAT[0:64, 4 * 512:5 * 512],
                LAT[0:64, 4 * 512:5 * 512], ALU.mult))
            lc5.add("tensor", lambda e, i, ib: e.matmul(
                pb[0][0:1, :], sb_ones[0:64, :], K2[0:64, 0:512],
                start=False, stop=True))
            lc5.add("vector", lambda e, i, ib: e.tensor_reduce(
                rows[:1, 1024:1025], pb[0][0:1, :],
                mybir.AxisListType.X, ALU.max))
            lc5.add("vector", lambda e, i, ib: e.tensor_tensor(
                sb_mrow[:1, 16:17], sb_mrow[:1, 16:17],
                rows[:1, 1024:1025], ALU.max))
            for sub in range(4):
                def vM(e, i, ib, sub=sub):
                    mm = None
                    for vg in range(4):
                        for lc in range(4):
                            mm = e.matmul(
                                pb[4 + vg][:, :],
                                LAT[:, lc * 512 + sub * 128:lc * 512 + sub * 128 + 128],
                                VWT[:, lc * 2048 + vg * 512:lc * 2048 + (vg + 1) * 512],
                                start=(lc == 0), stop=(lc == 3))
                    return mm
                lc5.add("tensor", vM)
                def vE(e, i, ib):
                    last = None
                    for q in range(4):
                        last = e.copy(VOUT[:, q * 512:(q + 1) * 512], pb[4 + q])
                    return last
                lc5.add("scalar", vE)
                lc5.add("sync", lambda e, i, ib, sub=sub: e.dma_start(
                    out=vscr[:, :, :].rearrange("a p m -> a p m")[ts(4 * i + sub, 1)]
                    .squeeze(0),
                    in_=VOUT), kind="w")
            lc5.emit(0)

        def pmfin():
            ch = Chain(S["e0"], S["e1"], S["e2"])
            ch.add("scalar", lambda e: e.activation(rows[:1, 0:32], sb_mrow[:1, 0:32],
                                                    AF.Sqrt))
            ch.add("vector", lambda e: e.tensor_scalar(
                rows[:1, 0:16], rows[:1, 0:16],
                rows[:1, 16:17], -SC192, ALU.mult, ALU.mult))
            ch.add("vector", lambda e: e.tensor_copy(rowsb[:1, 0:16],
                                                     rows[:1, 0:16]))
            ch.add("tensor", lambda e: e.matmul(pb[0][:, 0:16], sb_onesr[:],
                                                rowsb[:1, 0:16], start=True, stop=True))
            ch.add("scalar", lambda e: e.copy(sb_mtil[:, 0:16], pb[0][:, 0:16]))
            ch.emit()


        def p6():
            QN = aB(0, 512)
            QR = aB(512, 512)
            KT = aB(1024, 512)
            RT = aB(1536, 512)
            VT = aB(2048, 512)
            PT = aB(2560, 2048)
            MSK = aB(4608, 2048)
            RLB = aB(6656, 1024).bitcast(F32)

            def f(e):
                e.dma_start(out=MSK[:].rearrange("p (a t) -> p a t", a=4),
                            in_=G["mla_masks"].rearrange("(a p) t -> p a t", p=128)) \
                    .then_inc(S["e3"], 16)
                skb = nv(e, 4, lo=1, hi=48)
                cums = [nv(e, 5, lo=0, hi=48), nv(e, 6, lo=0, hi=48),
                        nv(e, 7, lo=0, hi=48)]
                for h in range(H):
                    for qt in range(NTE):
                        nhq = 3 * h + qt
                        bc = skb * h + cums[qt]
                        nkbm1 = nv(e, 1 + qt, lo=1, hi=16) - 1
                        pk = 5 * bc + (6 * nhq + 1) + 5 * nkbm1
                        e.wait_ge(S["f0"], pk + 4)
                        e.tensor_tensor(PT[:], PT[:], MSK[:], ALU.mult) \
                            .then_inc(S["f0"], 1)
            blk.gpsimd(f)

            def f(e):
                e.wait_ge(S["e3"], 16)
                skb = nv(e, 4, lo=1, hi=48)
                cums = [nv(e, 5, lo=0, hi=48), nv(e, 6, lo=0, hi=48),
                        nv(e, 7, lo=0, hi=48)]
                for h in range(H):
                    for qt in range(NTE):
                        nhq = 3 * h + qt
                        bc = skb * h + cums[qt]
                        nkbv = nv(e, 1 + qt, lo=1, hi=16)
                        with e.If(nhq > 0):
                            e.wait_ge(S["f0"], 5 * bc + 6 * nhq)
                        e.dma_start(out=QN[:], in_=qscr[h][:, qt * TT:(qt + 1) * TT]) \
                            .then_inc(S["f2"], 16)
                        e.dma_start(out=QR[0:64, :],
                                    in_=qscr[16 + h // 2]
                                    [64 * (h % 2):64 * (h % 2) + 64,
                                     qt * TT:(qt + 1) * TT]).then_inc(S["f2"], 16)
                        with e.Fori(0, nkbv) as i:
                            with e.If(i > 0):
                                e.wait_ge(S["f0"],
                                          5 * bc + (6 * nhq + 1) + 5 * i)
                            e.dma_start(out=KT[:],
                                        in_=kscr[h][:, ts(i, TT)]).then_inc(S["f1"], 16)
                            e.dma_start(out=RT[0:64, :],
                                        in_=latscr[4][0:64, ts(i, TT)]) \
                                .then_inc(S["f1"], 16)
                            e.dma_start(
                                out=VT[:].rearrange("p (a m) -> p a m", a=4),
                                in_=vscr[:, :, 128 * h:128 * h + 128]
                                [ts(i, 4)].rearrange("a p m -> p a m"),
                            ).then_inc(S["f1"], 16)
            blk.sync(f)

            def f(e):
                e.wait_ge(S["e3"], 16)
                skb = nv(e, 4, lo=1, hi=48)
                cums = [nv(e, 5, lo=0, hi=48), nv(e, 6, lo=0, hi=48),
                        nv(e, 7, lo=0, hi=48)]
                for h in range(H):
                    for qt in range(NTE):
                        nhq = 3 * h + qt
                        bc = skb * h + cums[qt]
                        base = 6 * nhq
                        nkbv = nv(e, 1 + qt, lo=1, hi=16)
                        nkbm1 = nkbv - 1
                        with e.If(nhq + 0 > 0):
                            e.wait_ge(S["f0"], 5 * bc + base)
                        e.matmul(pb[6][:, :], sb_zrow[:1, :], rowsb[:1, 0:512],
                                 start=True, stop=False)
                        e.matmul(pb[7][0:1, :], sb_zrow[:1, 0:1], rowsb[:1, 0:512],
                                 start=True, stop=False).then_inc(S["f0"], 1)

                        def smx(lo, cidx, iexpr):
                            e.wait_ge(S["f0"], cidx)
                            e.wait_ge(S["f1"], 48 * (bc + iexpr) + 48)
                            e.wait_ge(S["f2"], 32 * nhq + 32)
                            mm = None
                            for sub in (0, 1) if lo else (2, 3):
                                bk = pb[sub % 4]
                                e.matmul(bk[:, :],
                                         KT[:, sub * 128:(sub + 1) * 128],
                                         QN[:], start=True, stop=False)
                                mm = e.matmul(bk[:, :],
                                              RT[0:64, sub * 128:(sub + 1) * 128],
                                              QR[0:64, :], start=False, stop=True)
                            mm.then_inc(S["f0"], 1)

                        def lav(cidx, fin=False):
                            e.wait_ge(S["f0"], cidx)
                            mm = None
                            for sub in range(4):
                                e.matmul(pb[7][0:1, :], sb_ones[:],
                                         PT[:, sub * 512:(sub + 1) * 512],
                                         start=False, stop=(fin and sub == 3))
                                mm = e.matmul(pb[6][:, :],
                                              VT[:, sub * 128:(sub + 1) * 128],
                                              PT[:, sub * 512:(sub + 1) * 512],
                                              start=False, stop=(fin and sub == 3))
                            mm.then_inc(S["f0"], 1)

                        with e.If(nkbm1 > 0):
                            with e.Fori(0, nkbm1) as i:
                                cb = 5 * bc + (base + 1) + 5 * i
                                smx(True, cb, i)
                                smx(False, cb + 2, i)
                                lav(cb + 4)
                        pk = 5 * bc + (base + 1) + 5 * nkbm1
                        smx(True, pk, nkbm1)
                        smx(False, pk + 2, nkbm1)
                        lav(pk + 5, fin=True)
                        e.wait_ge(S["f0"], pk + 7)
                        e.matmul(pb[5][:, :], sb_onesr[:], rowsb2[:1, 0:512],
                                 start=True, stop=True).then_inc(S["f0"], 1)
            blk.tensor(f)

            def f(e):
                e.wait_ge(S["e3"], 16)
                skb = nv(e, 4, lo=1, hi=48)
                cums = [nv(e, 5, lo=0, hi=48), nv(e, 6, lo=0, hi=48),
                        nv(e, 7, lo=0, hi=48)]
                for h in range(H):
                    for qt in range(NTE):
                        nhq = 3 * h + qt
                        bc = skb * h + cums[qt]
                        base = 6 * nhq
                        nkbv = nv(e, 1 + qt, lo=1, hi=16)
                        nkbm1 = nkbv - 1
                        with e.If(nkbm1 > 0):
                            with e.Fori(0, nkbm1) as i:
                                cb = 5 * bc + (base + 1) + 5 * i
                                e.wait_ge(S["f0"], cb + 1)
                                e.activation(PT[:, 0:512], pb[0], AF.Exp,
                                             bias=sb_mtil[:, h:h + 1], scale=SC192)
                                e.activation(PT[:, 512:1024], pb[1], AF.Exp,
                                             bias=sb_mtil[:, h:h + 1], scale=SC192) \
                                    .then_inc(S["f0"], 1)
                                e.wait_ge(S["f0"], cb + 3)
                                e.activation(PT[:, 1024:1536], pb[2], AF.Exp,
                                             bias=sb_mtil[:, h:h + 1], scale=SC192)
                                e.activation(PT[:, 1536:2048], pb[3], AF.Exp,
                                             bias=sb_mtil[:, h:h + 1], scale=SC192) \
                                    .then_inc(S["f0"], 1)
                        pk = 5 * bc + (base + 1) + 5 * nkbm1
                        e.wait_ge(S["f0"], pk + 1)
                        e.activation(PT[:, 0:512], pb[0], AF.Exp,
                                     bias=sb_mtil[:, h:h + 1], scale=SC192)
                        e.activation(PT[:, 512:1024], pb[1], AF.Exp,
                                     bias=sb_mtil[:, h:h + 1], scale=SC192) \
                            .then_inc(S["f0"], 1)
                        e.wait_ge(S["f0"], pk + 3)
                        e.activation(PT[:, 1024:1536], pb[2], AF.Exp,
                                     bias=sb_mtil[:, h:h + 1], scale=SC192)
                        e.activation(PT[:, 1536:2048], pb[3], AF.Exp,
                                     bias=sb_mtil[:, h:h + 1], scale=SC192) \
                            .then_inc(S["f0"], 1)
                        e.wait_ge(S["f0"], pk + 8)
                        e.copy(RLB[:, 0:512], pb[5]).then_inc(S["f0"], 1)
            blk.scalar(f)

            def f(e):
                e.wait_ge(S["e3"], 16)
                skb = nv(e, 4, lo=1, hi=48)
                cums = [nv(e, 5, lo=0, hi=48), nv(e, 6, lo=0, hi=48),
                        nv(e, 7, lo=0, hi=48)]
                for h in range(H):
                    for qt in range(NTE):
                        nhq = 3 * h + qt
                        bc = skb * h + cums[qt]
                        base = 6 * nhq
                        nkbm1 = nv(e, 1 + qt, lo=1, hi=16) - 1
                        pk = 5 * bc + (base + 1) + 5 * nkbm1
                        e.wait_ge(S["f0"], pk + 6)
                        e.reciprocal(rows[:1, 0:512], pb[7][0:1, :])
                        e.tensor_copy(rowsb2[:1, 0:512], rows[:1, 0:512]) \
                            .then_inc(S["f0"], 1)
                        e.wait_ge(S["f0"], pk + 9)
                        e.tensor_tensor(aA(1536 * h + 512 * qt, 512), pb[6][:, :],
                                        RLB[:, 0:512], ALU.mult).then_inc(S["f0"], 1)
            blk.vector(f)


        def p7():
            ch = Chain(S["g0"], S["g1"], S["g2"])
            WTI7 = aB(0, 2048)
            for dch in range(DCH):
                ch.add("gpsimd", lambda e, dch=dch: e.dma_start(
                    out=WTI7[:].rearrange("p (d m) -> p d m", d=16),
                    in_=G["w_o"].rearrange("(d p) m -> p d m", p=128)
                    [:, :, dch * 128:(dch + 1) * 128]), kind="s")
                def wmm(e, dch=dch):
                    mm = None
                    for t in range(NTE):
                        for hc in range(DCH):
                            mm = e.matmul(pb[t][:, :],
                                          WTI7[:, hc * 128:(hc + 1) * 128],
                                          aA(hc * 1536 + t * 512, 512),
                                          start=(hc == 0), stop=(hc == DCH - 1))
                    return mm
                ch.add("tensor", wmm)
                def res(e, dch=dch):
                    last = None
                    for t in range(NTE):
                        xsl = sb_x[:, dch * EXT + t * 512:dch * EXT + (t + 1) * 512]
                        last = e.tensor_tensor(xsl, xsl, pb[t][:, :], ALU.add)
                    return last
                ch.add("vector", res)
            ch.emit()

        def p8():
            ch = Chain(S["q0"], S["q1"], S["q2"])
            NGT = (2 * FH) // 128
            NPT = (D // 128) * 4
            for it in range(NGT + NPT):
                par = it % 2
                WQ = aB(par * 8192, 2048)
                TQ = aB(par * 8192 + 2048, 2048)
                B2 = aB(par * 8192 + 4096, 2048)
                SCB = aB(par * 8192 + 6144, 64)
                RSF = aB(par * 8192 + 6208, 128).bitcast(F32)
                if it < NGT:
                    src_ap = G["wg"][it * 128:(it + 1) * 128, :]
                    dst_ap = wgq[it * 128:(it + 1) * 128, :]
                else:
                    k = it - NGT
                    r, jj = k // 4, k % 4
                    src_ap = G["wp"][r * 128:(r + 1) * 128, jj * 2048:(jj + 1) * 2048]
                    dst_ap = wpq[r * 128:(r + 1) * 128, jj * 2048:(jj + 1) * 2048]
                wcov = max(0, ch.nc_ - 2) if it >= 2 else None
                ch.add("gpsimd", lambda e, s=src_ap, WQ=WQ: e.dma_start(out=WQ, in_=s),
                       kind="s", wc=wcov)
                eng = "vector" if it % 2 == 0 else "gpsimd"
                ch.add("scalar", lambda e, WQ=WQ, TQ=TQ: e.activation(
                    TQ[:], WQ[:], AF.Abs))
                ch.add("vector", lambda e, TQ=TQ, RSF=RSF: e.tensor_reduce(
                    RSF[:, 0:32], TQ[:].rearrange("p (g w) -> p g w", g=32),
                    mybir.AxisListType.X, ALU.add))
                ch.add("vector", lambda e, RSF=RSF: e.tensor_scalar(
                    RSF[:, 32:64], RSF[:, 0:32], 1.0 / 64, 1e-8,
                    ALU.mult, ALU.max))
                ch.add("vector", lambda e, SCB=SCB, RSF=RSF: e.tensor_copy(
                    SCB[:, 0:32], RSF[:, 32:64]))
                ch.add("vector", lambda e, SCB=SCB, RSF=RSF: e.reciprocal(
                    RSF[:, 32:64], SCB[:, 0:32]))
                def qmul(e, WQ=WQ, TQ=TQ, RSF=RSF):
                    return e.tensor_tensor(TQ[:].rearrange("p (g w) -> p g w", g=32),
                                           WQ[:].rearrange("p (g w) -> p g w", g=32),
                                           RSF[:, 32:64].unsqueeze(2)
                                           .broadcast_to((128, 32, 64)), ALU.mult)
                ch.add(eng, qmul)
                ch.add("vector", lambda e, TQ=TQ, B2=B2: e.tensor_single_scalar(
                    B2[:], TQ[:], -0.5, ALU.is_lt))
                ch.add("vector", lambda e, TQ=TQ, B2=B2: e.scalar_tensor_tensor(
                    TQ[:], TQ[:], 0.5, B2[:], ALU.is_gt, ALU.subtract))
                def qwh(e, WQ=WQ, TQ=TQ, SCB=SCB):
                    return e.tensor_tensor(WQ[:].rearrange("p (g w) -> p g w", g=32),
                                           TQ[:].rearrange("p (g w) -> p g w", g=32),
                                           SCB[:, 0:32].unsqueeze(2)
                                           .broadcast_to((128, 32, 64)), ALU.mult)
                ch.add(eng, qwh)
                ch.add("sync", lambda e, d=dst_ap, WQ=WQ: e.dma_start(out=d, in_=WQ),
                       kind="h")
            ch.emit()

        def emit_ffn(tiles, sems):
            ch = Chain(S[sems[0]], S[sems[1]], S[sems[2]], S[sems[3]])
            XB = aA(0, 8192)
            X2 = aA(8192, 8192)
            GLG = [aA(0, 2048), aA(4096, 2048)]
            GLU = [aA(2048, 2048), aA(6144, 2048)]
            HT = aB(0, 1024)
            H2Q = aB(1024, 1024)
            SG = aB(2048, 1024)
            WPT = [aB(3072 + k * 1024, 1024) for k in range(2)]
            INVB = aB(5120, 1024).bitcast(F32)
            for tt in tiles:
                ch.add("gpsimd", lambda e, tt=tt: e.tensor_copy(
                    chv(XB, DCH),
                    sb_x[:].rearrange("p (c t) -> p c t", c=DCH)[:, :, tt * TT:(tt + 1) * TT]))
                ch.add("scalar", lambda e: e.activation(X2, XB, AF.Square))
                def stats(e):
                    for d in range(DCH):
                        e.matmul(pb[0][0:1, :], sb_ones[:], XB[:, d * 512:(d + 1) * 512],
                                 start=(d == 0), stop=(d == DCH - 1))
                    mm = None
                    for d in range(DCH):
                        mm = e.matmul(pb[0][32:33, :], sb_ones[:],
                                      X2[:, d * 512:(d + 1) * 512],
                                      start=(d == 0), stop=(d == DCH - 1),
                                      tile_position=(0, 32))
                    return mm
                ch.add("tensor", stats)
                def mv(e):
                    e.tensor_scalar_mul(rows[:1, 0:512], pb[0][0:1, :], 1.0 / D)
                    e.tensor_tensor(rows[:1, 512:1024], rows[:1, 0:512],
                                    rows[:1, 0:512], ALU.mult)
                    return e.scalar_tensor_tensor(rows[:1, 512:1024], pb[0][32:33, :],
                                                  1.0 / D, rows[:1, 512:1024],
                                                  ALU.mult, ALU.subtract)
                ch.add("vector", mv)
                ch.add("scalar", lambda e: e.activation(rows[:1, 1024:1536],
                                                        rows[:1, 512:1024],
                                                        AF.Ln, bias=EPS_LN))
                ch.add("scalar", lambda e: e.activation(rows[:1, 1024:1536],
                                                        rows[:1, 1024:1536],
                                                        AF.Exp, scale=-0.5))
                def ab_(e):
                    e.tensor_tensor(rows[:1, 0:512], rows[:1, 0:512],
                                    rows[:1, 1024:1536], ALU.mult)
                    return e.tensor_scalar_mul(rows[:1, 0:512], rows[:1, 0:512], -1.0)
                ch.add("vector", ab_)
                def bc(e):
                    e.matmul(pb[1][:, :], sb_onesr32[:], rows[:1, 1024:1536],
                             start=True, stop=True)
                    return e.matmul(pb[2][:, :], sb_onesr32[:], rows[:1, 0:512],
                                    start=True, stop=True)
                ch.add("tensor", bc)
                def cp(e):
                    e.copy(rows2q[:, 0:512], pb[1])
                    return e.copy(rows2q[:, 512:1024], pb[2])
                ch.add("scalar", cp)
                ch.add("gpsimd", lambda e: e.tensor_tensor(
                    chv(X2, DCH), chv(XB, DCH),
                    rows2q[:, 0:512].unsqueeze(1).broadcast_to((128, DCH, 512)),
                    ALU.mult))
                ch.add("vector", lambda e: e.tensor_tensor(
                    chv(X2, DCH), chv(X2, DCH),
                    rows2q[:, 512:1024].unsqueeze(1).broadcast_to((128, DCH, 512)),
                    ALU.add))
                hcn, jcn, gmmc, hmulc = {}, {}, {}, {}

                def emit_loads(jp):
                    par = jp % 2
                    def wldg(e, jp=jp, par=par):
                        last = None
                        for d in range(DCH):
                            last = e.dma_start(
                                out=GLG[par][:, d * 128:(d + 1) * 128],
                                in_=wgq[jp * 128:(jp + 1) * 128,
                                        d * 128:(d + 1) * 128],
                                transpose=True)
                            last.then_inc(S[sems[1]], 16)
                        return last
                    ch.add("sync", wldg, kind="h", n=16, wc=gmmc.get(jp - 2))
                    hcn[jp] = ch.nh
                    def wldu(e, jp=jp, par=par):
                        last = None
                        for d in range(DCH):
                            last = e.dma_start(
                                out=GLU[par][:, d * 128:(d + 1) * 128],
                                in_=wgq[FH + jp * 128:FH + (jp + 1) * 128,
                                        d * 128:(d + 1) * 128],
                                transpose=True)
                            last.then_inc(S[sems[3]], 16)
                        return last
                    ch.add("scalar", wldu, kind="j", n=16, wc=gmmc.get(jp - 2))
                    jcn[jp] = ch.nj

                def emit_hss(jp2, fin):
                    def hss(e, jp2=jp2, fin=fin):
                        par2 = jp2 % 2
                        return e.matmul(pb[3][0:1, :], sb_ones[:],
                                        H2Q[:, par2 * 512:par2 * 512 + 512],
                                        start=(jp2 == 0), stop=fin)
                    ch.add("tensor", hss, wc=hmulc[jp2], wh=0, wj=0)

                emit_loads(0)
                emit_loads(1)
                for jp in range(64):
                    par = jp % 2
                    gb, ub = 4 + par, 6 + par
                    def gmm(e, par=par, gb=gb, ub=ub):
                        for d in range(DCH):
                            e.matmul(pb[gb][:, :],
                                     GLG[par][:, d * 128:(d + 1) * 128],
                                     X2[:, d * 512:(d + 1) * 512],
                                     start=(d == 0), stop=(d == DCH - 1))
                        mm = None
                        for d in range(DCH):
                            mm = e.matmul(pb[ub][:, :],
                                          GLU[par][:, d * 128:(d + 1) * 128],
                                          X2[:, d * 512:(d + 1) * 512],
                                          start=(d == 0), stop=(d == DCH - 1))
                        return mm
                    ch.add("tensor", gmm, wc=hmulc.get(jp - 2),
                           wh=hcn[jp], wj=jcn[jp])
                    gmmc[jp] = ch.nc_
                    if jp + 2 < 64:
                        emit_loads(jp + 2)
                    ch.add("scalar", lambda e, par=par, gb=gb: e.activation(
                        SG[:, par * 512:par * 512 + 512], pb[gb], AF.Silu),
                        wh=0, wj=0)
                    if jp >= 2:
                        emit_hss(jp - 2, False)
                    def hmul2(e, par=par, ub=ub):
                        e.tensor_tensor(HT[:, par * 512:par * 512 + 512],
                                        SG[:, par * 512:par * 512 + 512],
                                        pb[ub], ALU.mult)
                        return e.tensor_tensor(
                            H2Q[:, par * 512:par * 512 + 512],
                            HT[:, par * 512:par * 512 + 512],
                            HT[:, par * 512:par * 512 + 512], ALU.mult)
                    ch.add("vector", hmul2, wh=0, wj=0)
                    hmulc[jp] = ch.nc_
                    ch.add("gpsimd", lambda e, jp=jp, par=par: e.dma_start(
                        out=hscr2[jp, :, :],
                        in_=HT[:, par * 512:par * 512 + 512]),
                        kind="s", wh=0, wj=0)
                emit_hss(62, False)
                emit_hss(63, True)
                ch.add("scalar", lambda e: e.activation(
                    rows[:1, 0:512], pb[3][0:1, :], AF.Ln, bias=EPS_FFN,
                    scale=1.0 / FH))
                ch.add("scalar", lambda e: e.activation(
                    rows[:1, 0:512], rows[:1, 0:512], AF.Exp, scale=-0.5))
                ch.add("scalar", lambda e: e.copy(rowsb2[:1, 0:512], rows[:1, 0:512]))
                ch.add("tensor", lambda e: e.matmul(
                    pb[3][:, :], sb_onesr[:], rowsb2[:1, 0:512], start=True, stop=True))
                ch.add("scalar", lambda e: e.copy(INVB[:, 0:512], pb[3]))
                for dh in range(2):
                    ybase = ch.nc_
                    ycnt, ymmc = {}, {}

                    def yld_emit(jc, dh=dh, ybase=ybase, ycnt=ycnt, ymmc=ymmc):
                        par = jc % 2
                        def yld(e, jc=jc, dh=dh, par=par):
                            return e.dma_start(
                                out=WPT[par][:],
                                in_=wpq[dh * 1024:(dh + 1) * 1024,
                                        jc * 128:(jc + 1) * 128],
                                transpose=True)
                        def hld(e, jc=jc, par=par):
                            return e.dma_start(
                                out=HT[:, par * 512:par * 512 + 512],
                                in_=hscr2[jc, :, :])
                        wcov = ymmc.get(jc - 2, ybase)
                        if par == 0:
                            ch.add("sync", yld, kind="h", wc=wcov)
                            ch.add("scalar", hld, kind="j", wc=wcov)
                        else:
                            ch.add("scalar", yld, kind="j", wc=wcov)
                            ch.add("sync", hld, kind="h", wc=wcov)
                        ycnt[jc] = (ch.nh, ch.nj)

                    yld_emit(0)
                    yld_emit(1)
                    for jc in range(64):
                        par = jc % 2
                        def ymm(e, jc=jc, dh=dh, par=par):
                            mm = None
                            for oc in range(8):
                                mm = e.matmul(
                                    pb[oc][:, :],
                                    WPT[par][:, oc * 128:(oc + 1) * 128],
                                    HT[:, par * 512:par * 512 + 512],
                                    start=(jc == 0), stop=(jc == 63))
                            return mm
                        ch.add("tensor", ymm, wh=ycnt[jc][0], wj=ycnt[jc][1])
                        ymmc[jc] = ch.nc_
                        if jc + 2 < 64:
                            yld_emit(jc + 2)
                    def yev(e, dh=dh, tt=tt):
                        last = None
                        for oc in range(8):
                            dchk = dh * 8 + oc
                            xsl = sb_x[:, dchk * EXT + tt * 512:
                                       dchk * EXT + (tt + 1) * 512]
                            e.tensor_tensor(rows2q[:, 0:512], pb[oc][:, :],
                                            INVB[:, 0:512], ALU.mult)
                            last = e.tensor_tensor(xsl, xsl, rows2q[:, 0:512],
                                                   ALU.add)
                        return last
                    ch.add("vector", yev, wh=0, wj=0)
            ch.emit()

        def p9():
            emit_ffn(list(range(NTE)), ("n0", "n1", "n2", "n3"))

        def p11():
            emit_ffn([1, 2], ("w0", "w1", "w2", "w3"))


        def p10():
            ch = Chain(S["z0"], S["z1"], S["z2"])
            XB = aA(0, 8192)
            H3 = aA(8192, 8192)
            WTI0 = aA(16384, 2048)
            WVT = aA(16384, 8192)
            QOUT = aA(24320, 512)
            ROPB = aA(24832, 512)
            COSB = aA(25344, 1024).bitcast(F32)
            SINB = aA(26368, 1024).bitcast(F32)
            PT = aA(0, 2048)
            RLB = aA(2048, 1024).bitcast(F32)
            def ogt(q2, qh):
                if q2 == 0:
                    return aA(4096 + qh * 512, 512)
                return aA(12288 + qh * 512, 512) if qh < 8 else \
                    aA(18432 + (qh - 8) * 512, 512)
            KST = aB(1024, 6144)
            VST = aB(7168, 6144)
            GMSK = aB(13312, 8192)
            BVB = rowsb2

            ch.add("gpsimd", lambda e: e.dma_start(
                out=GMSK[:].rearrange("p (a t) -> p a t", a=16),
                in_=G["gqa_masks"].rearrange("(a p) t -> p a t", p=128)), kind="s")
            ch.add("gpsimd", lambda e: e.dma_start(out=BVB[:1, :],
                                                   in_=G["bv_row"]), kind="s")
            for tt in range(NTE):
                ch.add("gpsimd", lambda e, tt=tt: e.tensor_copy(
                    chv(XB, DCH),
                    sb_x[:].rearrange("p (c t) -> p c t", c=DCH)[:, :, tt * TT:(tt + 1) * TT]))
                ch.add("scalar", lambda e: e.activation(H3, XB, AF.Square))
                def stats(e):
                    for d in range(DCH):
                        e.matmul(pb[0][0:1, :], sb_ones[:], XB[:, d * 512:(d + 1) * 512],
                                 start=(d == 0), stop=(d == DCH - 1))
                    mm = None
                    for d in range(DCH):
                        mm = e.matmul(pb[0][32:33, :], sb_ones[:],
                                      H3[:, d * 512:(d + 1) * 512],
                                      start=(d == 0), stop=(d == DCH - 1),
                                      tile_position=(0, 32))
                    return mm
                ch.add("tensor", stats)
                def mv(e):
                    e.tensor_scalar_mul(rows[:1, 0:512], pb[0][0:1, :], 1.0 / D)
                    e.tensor_tensor(rows[:1, 512:1024], rows[:1, 0:512],
                                    rows[:1, 0:512], ALU.mult)
                    return e.scalar_tensor_tensor(rows[:1, 512:1024], pb[0][32:33, :],
                                                  1.0 / D, rows[:1, 512:1024],
                                                  ALU.mult, ALU.subtract)
                ch.add("vector", mv)
                ch.add("scalar", lambda e: e.activation(rows[:1, 1024:1536],
                                                        rows[:1, 512:1024],
                                                        AF.Ln, bias=EPS_LN))
                ch.add("scalar", lambda e: e.activation(rows[:1, 1024:1536],
                                                        rows[:1, 1024:1536],
                                                        AF.Exp, scale=-0.5))
                def ab_(e):
                    e.tensor_tensor(rows[:1, 0:512], rows[:1, 0:512],
                                    rows[:1, 1024:1536], ALU.mult)
                    return e.tensor_scalar_mul(rows[:1, 0:512], rows[:1, 0:512], -1.0)
                ch.add("vector", ab_)
                def bc(e):
                    e.matmul(pb[1][:, :], sb_onesr32[:], rows[:1, 1024:1536],
                             start=True, stop=True)
                    return e.matmul(pb[2][:, :], sb_onesr32[:], rows[:1, 0:512],
                                    start=True, stop=True)
                ch.add("tensor", bc)
                def cp(e):
                    e.copy(rows2q[:, 0:512], pb[1])
                    return e.copy(rows2q[:, 512:1024], pb[2])
                ch.add("scalar", cp)
                ch.add("gpsimd", lambda e: e.tensor_tensor(
                    chv(H3, DCH), chv(XB, DCH),
                    rows2q[:, 0:512].unsqueeze(1).broadcast_to((128, DCH, 512)),
                    ALU.mult))
                ch.add("vector", lambda e: e.tensor_tensor(
                    chv(H3, DCH), chv(H3, DCH),
                    rows2q[:, 512:1024].unsqueeze(1).broadcast_to((128, DCH, 512)),
                    ALU.add))
                ch.add("gpsimd", lambda e, tt=tt: e.dma_start(
                    out=COSB[:, 0:512], in_=G["gcos_e"][:, tt * TT:(tt + 1) * TT]), kind="s")
                ch.add("gpsimd", lambda e, tt=tt: e.dma_start(
                    out=SINB[:, 0:512], in_=G["gsin_e"][:, tt * TT:(tt + 1) * TT]), kind="s")
                for g in range(GNKV):
                    ch.add("gpsimd", lambda e, g=g: e.dma_start(
                        out=WTI0[:].rearrange("p (d m) -> p d m", d=16),
                        in_=G["Wk_p"].rearrange("(d p) m -> p d m", p=128)
                        [:, :, g * 128:(g + 1) * 128]), kind="s")
                    def kmm(e):
                        mm = None
                        for d in range(DCH):
                            mm = e.matmul(pb[6][:, :], WTI0[:, d * 128:(d + 1) * 128],
                                          H3[:, d * 512:(d + 1) * 512],
                                          start=(d == 0), stop=(d == DCH - 1))
                        return mm
                    ch.add("tensor", kmm)
                    ch.add("vector", lambda e, g=g: e.tensor_scalar_add(
                        ROPB, pb[6], sb_bkp[:, g:g + 1]))
                    ch.add("tensor", lambda e: e.matmul(
                        pb[1][:, :], sb_swap128[:, :], ROPB, start=True, stop=True))
                    def kcomb(e, g=g, tt=tt):
                        e.tensor_tensor(rows2q[:, 0:512], ROPB, COSB[:, 0:512], ALU.mult)
                        e.tensor_tensor(rows2q[:, 512:1024], pb[1][:, :],
                                        SINB[:, 0:512], ALU.mult)
                        e.tensor_tensor(KST[:, g * 1536 + tt * 512:
                                            g * 1536 + (tt + 1) * 512],
                                        rows2q[:, 0:512], rows2q[:, 512:1024], ALU.add)
                        e.tensor_tensor(QOUT, KST[:, g * 1536 + tt * 512:
                                                  g * 1536 + (tt + 1) * 512],
                                        KST[:, g * 1536 + tt * 512:
                                            g * 1536 + (tt + 1) * 512], ALU.mult)
                        return e.nop()
                    ch.add("vector", kcomb)
                    ch.add("tensor", lambda e: e.matmul(
                        pb[3][0:1, :], sb_ones[:], QOUT, start=True, stop=True))
                    ch.add("vector", lambda e: e.tensor_reduce(
                        rows[:1, 1024:1025], pb[3][0:1, :],
                        mybir.AxisListType.X, ALU.max))
                    ch.add("vector", lambda e: e.tensor_tensor(
                        sb_mrow[:1, 48:49], sb_mrow[:1, 48:49],
                        rows[:1, 1024:1025], ALU.max))
                ch.add("gpsimd", lambda e: e.dma_start(
                    out=WVT[:].rearrange("p (d m) -> p d m", d=16),
                    in_=G["Wv"].rearrange("(d p) m -> p d m", p=128)), kind="s")
                def vmm(e, tt=tt):
                    mm = None
                    for sub in range(4):
                        for d in range(DCH):
                            e.matmul(pb[sub][:, :],
                                     H3[:, d * 512 + sub * 128:d * 512 + sub * 128 + 128],
                                     WVT[:, d * 512:(d + 1) * 512],
                                     start=(d == 0), stop=False)
                        mm = e.matmul(pb[sub][:, :], sb_onesr[:], BVB[:1, 0:512],
                                      start=False, stop=True)
                    return mm
                ch.add("tensor", vmm)
                def vev(e, tt=tt):
                    last = None
                    for sub in range(4):
                        last = e.copy(VST[:, (tt * 4 + sub) * 512:
                                          (tt * 4 + sub + 1) * 512], pb[sub])
                    return last
                ch.add("scalar", vev)
                if tt >= 1:
                    for qh in range(GNQ):
                        ch.add("gpsimd", lambda e, qh=qh: e.dma_start(
                            out=WTI0[:].rearrange("p (d m) -> p d m", d=16),
                            in_=G["Wq_p"].rearrange("(d p) m -> p d m", p=128)
                            [:, :, qh * 128:(qh + 1) * 128]), kind="s")
                        def qmm(e):
                            mm = None
                            for d in range(DCH):
                                mm = e.matmul(pb[6][:, :],
                                              WTI0[:, d * 128:(d + 1) * 128],
                                              H3[:, d * 512:(d + 1) * 512],
                                              start=(d == 0), stop=(d == DCH - 1))
                            return mm
                        ch.add("tensor", qmm)
                        ch.add("vector", lambda e, qh=qh: e.tensor_scalar_add(
                            ROPB, pb[6], sb_bqp[:, qh:qh + 1]))
                        ch.add("tensor", lambda e: e.matmul(
                            pb[1][:, :], sb_swap128[:, :], ROPB, start=True, stop=True))
                        def qcomb(e):
                            e.tensor_tensor(rows2q[:, 0:512], ROPB, COSB[:, 0:512],
                                            ALU.mult)
                            e.tensor_tensor(rows2q[:, 512:1024], pb[1][:, :],
                                            SINB[:, 0:512], ALU.mult)
                            e.tensor_tensor(QOUT, rows2q[:, 0:512],
                                            rows2q[:, 512:1024], ALU.add)
                            return e.tensor_tensor(rows2q[:, 0:512], QOUT, QOUT,
                                                   ALU.mult)
                        ch.add("vector", qcomb)
                        ch.add("tensor", lambda e: e.matmul(
                            pb[3][0:1, :], sb_ones[:], rows2q_bf_hack(e),
                            start=True, stop=True)) if False else None
                        ch.add("scalar", lambda e: e.copy(ROPB, rows2q[:, 0:512]))
                        ch.add("tensor", lambda e: e.matmul(
                            pb[3][0:1, :], sb_ones[:], ROPB, start=True, stop=True))
                        ch.add("vector", lambda e: e.tensor_reduce(
                            rows[:1, 1024:1025], pb[3][0:1, :],
                            mybir.AxisListType.X, ALU.max))
                        ch.add("vector", lambda e, qh=qh: e.tensor_tensor(
                            sb_mrow[:1, 32 + qh:33 + qh],
                            sb_mrow[:1, 32 + qh:33 + qh],
                            rows[:1, 1024:1025], ALU.max))
                        ch.add("sync", lambda e, qh=qh, tt=tt: e.dma_start(
                            out=qscr[qh][:, (tt - 1) * TT:tt * TT], in_=QOUT), kind="h")
            ch.add("scalar", lambda e: e.activation(rows[:1, 0:32],
                                                    sb_mrow[:1, 32:64], AF.Sqrt))
            ch.add("vector", lambda e: e.tensor_scalar(
                rows[:1, 0:16], rows[:1, 0:16],
                rows[:1, 16:17], -SC128, ALU.mult, ALU.mult))
            ch.add("vector", lambda e: e.tensor_copy(rowsb[:1, 0:16],
                                                     rows[:1, 0:16]))
            ch.add("tensor", lambda e: e.matmul(pb[0][:, 0:16], sb_onesr[:],
                                                rowsb[:1, 0:16], start=True, stop=True))
            ch.add("scalar", lambda e: e.copy(sb_mtil[:, 16:32], pb[0][:, 0:16]))
            for qh in range(GNQ):
                kvh = qh // 4
                for q2 in range(2):
                    ch.add("sync", lambda e, qh=qh, q2=q2: e.dma_start(
                        out=QOUT, in_=qscr[qh][:, q2 * TT:(q2 + 1) * TT]), kind="h")
                    def zmm(e):
                        e.matmul(pb[6][:, :], sb_zrow[:1, :], rowsb[:1, 0:512],
                                 start=True, stop=False)
                        return e.matmul(pb[7][0:1, :], sb_zrow[:1, 0:1],
                                        rowsb[:1, 0:512], start=True, stop=False)
                    ch.add("tensor", zmm)
                    for kb in range(2):
                        def sm(e, qh=qh, q2=q2, kb=kb, kvh=kvh):
                            mm = None
                            for sub in range(4):
                                mm = e.matmul(
                                    pb[sub][:, :],
                                    KST[:, kvh * 1536 + (q2 + kb) * 512 + sub * 128:
                                        kvh * 1536 + (q2 + kb) * 512 + sub * 128 + 128],
                                    QOUT, start=True, stop=True)
                            return mm
                        ch.add("tensor", sm)
                        def gexp(e, qh=qh):
                            last = None
                            for sub in range(4):
                                last = e.activation(
                                    PT[:, sub * 512:(sub + 1) * 512], pb[sub], AF.Exp,
                                    bias=sb_mtil[:, 16 + qh:17 + qh], scale=SC128)
                            return last
                        ch.add("scalar", gexp)
                        ch.add("gpsimd", lambda e, q2=q2, kb=kb: e.tensor_tensor(
                            PT[:], PT[:],
                            GMSK[:, (q2 * 2 + kb) * 2048:(q2 * 2 + kb + 1) * 2048],
                            ALU.mult))
                        def lav(e, q2=q2, kb=kb, kvh=kvh):
                            fin = (kb == 1)
                            mm = None
                            for sub in range(4):
                                e.matmul(pb[7][0:1, :], sb_ones[:],
                                         PT[:, sub * 512:(sub + 1) * 512],
                                         start=False, stop=(fin and sub == 3))
                                mm = e.matmul(
                                    pb[6][:, :],
                                    VST[:, ((q2 + kb) * 4 + sub) * 512 + kvh * 128:
                                        ((q2 + kb) * 4 + sub) * 512 + kvh * 128 + 128],
                                    PT[:, sub * 512:(sub + 1) * 512],
                                    start=False, stop=(fin and sub == 3))
                            return mm
                        ch.add("tensor", lav)
                    def rl(e):
                        e.reciprocal(rows[:1, 0:512], pb[7][0:1, :])
                        return e.tensor_copy(rowsb2[:1, 0:512], rows[:1, 0:512])
                    ch.add("vector", rl)
                    ch.add("tensor", lambda e: e.matmul(
                        pb[5][:, :], sb_onesr[:], rowsb2[:1, 0:512],
                        start=True, stop=True))
                    ch.add("scalar", lambda e: e.copy(RLB[:, 0:512], pb[5]))
                    ch.add("vector", lambda e, qh=qh, q2=q2: e.tensor_tensor(
                        ogt(q2, qh)[:], pb[6][:, :],
                        RLB[:, 0:512], ALU.mult))
            for dch in range(DCH):
                ch.add("gpsimd", lambda e, dch=dch: e.dma_start(
                    out=WTI0[:].rearrange("p (d m) -> p d m", d=16),
                    in_=G["Wo_b"].rearrange("(d p) m -> p d m", p=128)
                    [:, :, dch * 128:(dch + 1) * 128]), kind="s")
                def wmm(e, dch=dch):
                    mm = None
                    for q2 in range(2):
                        for hc in range(DCH):
                            mm = e.matmul(pb[q2][:, :],
                                          WTI0[:, hc * 128:(hc + 1) * 128],
                                          ogt(q2, hc)[:],
                                          start=(hc == 0), stop=(hc == DCH - 1))
                    return mm
                ch.add("tensor", wmm)
                def res(e, dch=dch):
                    last = None
                    for q2 in range(2):
                        xsl = sb_x[:, dch * EXT + (q2 + 1) * 512:
                                   dch * EXT + (q2 + 2) * 512]
                        last = e.scalar_tensor_tensor(xsl, pb[q2][:, :],
                                                      sb_bop[:, dch:dch + 1], xsl,
                                                      ALU.add, ALU.add)
                    return last
                ch.add("vector", res)
            ch.emit()

        def p12():
            def f(e):
                e.dma_start(
                    out=yout.rearrange("(c p) t -> p c t", p=128),
                    in_=sb_x[:].rearrange("p (c t) -> p c t", c=DCH)[:, :, HALO:EXT],
                ).then_inc(S["o0"], 16)
                e.wait_ge(S["o0"], 16)
            blk.sync(f)

        p0()
        barrier()
        if upto >= 1:
            p1()
            barrier()
        if upto >= 2:
            p2()
            barrier()
        if upto >= 3:
            p5()
            barrier()
            pmfin()
            barrier()
        if upto >= 4:
            p6()
            barrier()
        if upto >= 5:
            p7()
            barrier()
        if upto >= 6:
            p8()
            barrier()
        if upto >= 7:
            p9()
            barrier()
        if upto >= 8:
            p10()
            barrier()
        if upto >= 9:
            p11()
            barrier()
        p12()

    return nc, yout


def kernel(**inputs):
    nc, _ = build_program(BUILD_UPTO)
    nc.compile()
    in_maps = [make_inputs(c, inputs) for c in range(NCORE)]
    res = run_bass_kernel_spmd(nc, in_maps, core_ids=list(range(NCORE)))
    out = np.zeros((1, L, D), np.float32)
    for c in range(NCORE):
        out[0, SEG * c:SEG * (c + 1), :] = np.asarray(res.results[c]["yout"]).T
    return out


# version 08 (final)
# speedup vs baseline: 1.2997x; 1.2997x over previous
import math
from contextlib import ExitStack

import numpy as np

import concourse.bass as bass
import concourse.bacc as bacc
import concourse.mybir as mybir
from concourse.bass_utils import run_bass_kernel_spmd

F32 = mybir.dt.float32
BF16 = mybir.dt.bfloat16
I32 = mybir.dt.int32
AF = mybir.ActivationFunctionType
ALU = mybir.AluOpType
ts = bass.ts

NCORE = 8
L = 8192
D = 2048
SEG = L // NCORE
HALO = 512
EXT = SEG + HALO
TT = 512
NTE = EXT // TT
NTO = SEG // TT
DCH = D // 128

H = 16
DR = 64
QL = 1536
KL = 512
QLC = QL // 128
KLC = KL // 128
SC192 = 1.0 / math.sqrt(192.0)
SC128 = 1.0 / math.sqrt(128.0)
EPS_LN = 1e-5
EPS_RMS = 1e-6

GNQ, GNKV, GHD = 16, 4, 128
WINDOW = 512

FH = 8192
EPS_FFN = float(np.finfo(np.float32).eps)

BUILD_UPTO = 99


def _yarn_tables(positions):
    pos2 = np.arange(0, DR, 2, dtype=np.float64)
    inv_freq = 1.0 / (10000.0 ** (pos2 / DR))
    wavelen = 2.0 * np.pi / inv_freq
    ramp = np.clip((8192.0 / wavelen - 1.0) / 3.0, 0.0, 1.0)
    inv_freq = inv_freq * (1.0 - ramp) + inv_freq / 16.0 * ramp
    freqs = np.outer(inv_freq, positions.astype(np.float64))
    attn = 0.1 * math.log(16.0) + 1.0
    cos = np.cos(freqs) * attn
    sin = np.sin(freqs) * attn
    return (np.concatenate([cos, cos], 0).astype(np.float32),
            np.concatenate([-sin, sin], 0).astype(np.float32))


def _gqa_tables(positions):
    half = GHD // 2
    inv_freq = 1.0 / (10000.0 ** (2.0 * np.arange(half, dtype=np.float64) / GHD))
    freqs = np.outer(inv_freq, positions.astype(np.float64))
    return (np.concatenate([np.cos(freqs), np.cos(freqs)], 0).astype(np.float32),
            np.concatenate([-np.sin(freqs), np.sin(freqs)], 0).astype(np.float32))


def _swap_mat(block):
    s = np.zeros((128, 128), np.float32)
    hb = block // 2
    for b in range(128 // block):
        for m in range(block):
            s[b * block + (m + hb) % block, b * block + m] = 1.0
    return s


def _mla_diag_masks():
    k = np.arange(128)[:, None]
    q = np.arange(512)[None, :]
    return np.stack([(q >= k + 128 * j).astype(np.float32) for j in range(4)])


def _gqa_masks(core):
    base = SEG * core - HALO
    out = np.zeros((2, 2, 4, 128, 512), np.float32)
    k = np.arange(128)[:, None]
    q = np.arange(512)[None, :]
    for qt in range(2):
        qpos = base + HALO + 512 * qt + q
        for kb in range(2):
            for sub in range(4):
                kpos = base + 512 * (qt + kb) + 128 * sub + k
                ok = (qpos - kpos >= 0) & (qpos - kpos <= WINDOW) & (kpos >= 0)
                out[qt, kb, sub] = ok.astype(np.float32)
    return out


def _chunked(vec, nch):
    return np.ascontiguousarray(np.asarray(vec).reshape(nch, 128).T).astype(np.float32)


def make_inputs(core, A):
    x = np.asarray(A["L0_attn_a_x"])[0]
    xT = np.ascontiguousarray(x.T).astype(np.float32)
    lo = SEG * core - HALO
    if lo < 0:
        xe = np.concatenate([np.zeros((D, HALO), np.float32),
                             xT[:, :SEG * (core + 1)]], axis=1)
    else:
        xe = np.ascontiguousarray(xT[:, lo:SEG * (core + 1)])

    w_qb = np.asarray(A["L0_attn_a_w_qb"])
    nope_cols = np.concatenate([np.arange(h * 192, h * 192 + 128) for h in range(H)])
    rope_cols = np.concatenate([np.arange(h * 192 + 128, (h + 1) * 192) for h in range(H)])
    w_qb_perm = np.ascontiguousarray(
        np.concatenate([w_qb[:, nope_cols], w_qb[:, rope_cols]], axis=1))

    w_kvb = np.asarray(A["L0_attn_a_w_kvb"])
    kcols = np.concatenate([np.arange(h * 256, h * 256 + 128) for h in range(H)])
    vcols = np.concatenate([np.arange(h * 256 + 128, (h + 1) * 256) for h in range(H)])
    w_kvb_perm = np.ascontiguousarray(
        np.concatenate([w_kvb[:, kcols], w_kvb[:, vcols]], axis=1))

    perm = np.concatenate([np.concatenate([np.arange(h * GHD, h * GHD + GHD, 2),
                                           np.arange(h * GHD + 1, h * GHD + GHD, 2)])
                           for h in range(GNQ)])
    permk = perm[:GNKV * GHD]
    Wq = np.asarray(A["L0_attn_b_Wq"])[:, perm]
    bq = np.asarray(A["L0_attn_b_bq"])[perm]
    Wk = np.asarray(A["L0_attn_b_Wk"])[:, permk]
    bk = np.asarray(A["L0_attn_b_bk"])[permk]

    pos_ext = np.arange(lo, SEG * (core + 1))
    pos_ext = np.where(pos_ext < 0, 0, pos_ext)
    ycos, ysin = _yarn_tables(np.arange(L))
    ycos_e, ysin_e = _yarn_tables(pos_ext)
    gcos_e, gsin_e = _gqa_tables(pos_ext)

    meta = np.zeros((1, 16), np.int32)
    meta[0, 0] = 2 * core + 2
    for qt in range(3):
        meta[0, 1 + qt] = max(1, 2 * core + qt)
    meta[0, 4] = max(1, 2 * core) + max(1, 2 * core + 1) + (2 * core + 2)
    meta[0, 5] = 0
    meta[0, 6] = max(1, 2 * core)
    meta[0, 7] = max(1, 2 * core) + max(1, 2 * core + 1)

    return {
        "x_ext": xe,
        "x_full": xT,
        "meta": meta,
        "w_qa": np.ascontiguousarray(A["L0_attn_a_w_qa"]).astype(np.float32),
        "w_qb_p": w_qb_perm.astype(np.float32),
        "w_kva": np.ascontiguousarray(A["L0_attn_a_w_kva"]).astype(np.float32),
        "w_kvb_p": w_kvb_perm.astype(np.float32),
        "w_o": np.ascontiguousarray(A["L0_attn_a_w_o"]).astype(np.float32),
        "q_norm_w": _chunked(A["L0_attn_a_q_norm_w"], QLC),
        "kv_norm_w": _chunked(A["L0_attn_a_kv_norm_w"], KLC),
        "Wq_p": np.ascontiguousarray(Wq).astype(np.float32),
        "bq_p": _chunked(bq, GNQ),
        "Wk_p": np.ascontiguousarray(Wk).astype(np.float32),
        "bk_p": _chunked(bk, GNKV),
        "Wv": np.ascontiguousarray(A["L0_attn_b_Wv"]).astype(np.float32),
        "bv_row": np.asarray(A["L0_attn_b_bv"]).reshape(1, 512).astype(np.float32),
        "Wo_b": np.ascontiguousarray(A["L0_attn_b_Wo"]).astype(np.float32),
        "bo_p": _chunked(A["L0_attn_b_bo"], DCH),
        "wg": np.ascontiguousarray(np.asarray(A["L0_ffn_w_gate_up"])).astype(np.float32),
        "wp": np.ascontiguousarray(np.asarray(A["L0_ffn_w_proj"])).astype(np.float32),
        "ycos": ycos, "ysin": ysin,
        "ycos_e": np.ascontiguousarray(np.concatenate([ycos_e, ycos_e], 0)),
        "ysin_e": np.ascontiguousarray(np.concatenate([ysin_e, ysin_e], 0)),
        "gcos_e": gcos_e, "gsin_e": gsin_e,
        "swap64": _swap_mat(64), "swap128": _swap_mat(128),
        "mla_masks": _mla_diag_masks().reshape(4 * 128, 512).astype(np.float32),
        "gqa_masks": _gqa_masks(core).reshape(16 * 128, 512).astype(np.float32),
    }


def build_program(upto=99):
    nc = bacc.Bacc("TRN2", target_bir_lowering=False)
    nc.detect_race_conditions = False

    def dt_in(name, shape, dt=F32):
        return nc.dram_tensor(name, shape, dt, kind="ExternalInput").ap()

    def dt_scr(name, shape, dt=BF16):
        return nc.dram_tensor(name, shape, dt).ap()

    G = {}
    for nm, sh, dt in [
        ("x_ext", [D, EXT], F32), ("x_full", [D, L], F32), ("meta", [1, 16], I32),
        ("w_qa", [D, QL], F32), ("w_qb_p", [QL, 3072], F32),
        ("w_kva", [D, 576], F32), ("w_kvb_p", [KL, 4096], F32), ("w_o", [D, D], F32),
        ("q_norm_w", [128, QLC], F32), ("kv_norm_w", [128, KLC], F32),
        ("Wq_p", [D, D], F32), ("bq_p", [128, GNQ], F32),
        ("Wk_p", [D, 512], F32), ("bk_p", [128, GNKV], F32),
        ("Wv", [D, 512], F32), ("bv_row", [1, 512], F32),
        ("Wo_b", [D, D], F32), ("bo_p", [128, DCH], F32),
        ("wg", [2 * FH, D], F32), ("wp", [D, FH], F32),
        ("ycos", [64, L], F32), ("ysin", [64, L], F32),
        ("ycos_e", [128, EXT], F32), ("ysin_e", [128, EXT], F32),
        ("gcos_e", [128, EXT], F32), ("gsin_e", [128, EXT], F32),
        ("swap64", [128, 128], F32), ("swap128", [128, 128], F32),
        ("mla_masks", [4 * 128, 512], F32), ("gqa_masks", [16 * 128, 512], F32),
    ]:
        G[nm] = dt_in(nm, sh, dt)

    yout = nc.dram_tensor("yout", [D, SEG], F32, kind="ExternalOutput").ap()

    latscr = dt_scr("latscr", [5, 128, L])
    kscr = dt_scr("kscr", [H, 128, L])
    vscr = dt_scr("vscr", [L // 128, 128, H * 128])
    qscr = dt_scr("qscr", [24, 128, EXT])
    wgq = dt_scr("wgq", [2 * FH, D])
    hscr2 = dt_scr("hscr2", [FH // 128, 128, TT])
    wpq = dt_scr("wpq", [D, FH])

    es = ExitStack()
    with es:
        SB = lambda name, shape, dt=BF16: es.enter_context(nc.sbuf_tensor(name, shape, dt))
        SEM = lambda name: es.enter_context(nc.semaphore(name))

        sb_x = SB("sb_x", [128, DCH * EXT], F32)
        sb_meta = SB("sb_meta", [1, 16], I32)
        sb_ones = SB("sb_ones", [128, 1], BF16)
        sb_onesr = SB("sb_onesr", [1, 128], BF16)
        sb_onesr32 = SB("sb_onesr32", [1, 128], F32)
        sb_zrow = SB("sb_zrow", [1, 128], BF16)
        sb_qnw = SB("sb_qnw", [128, QLC], F32)
        sb_bqp = SB("sb_bqp", [128, GNQ], F32)
        sb_bkp = SB("sb_bkp", [128, GNKV], F32)
        sb_bop = SB("sb_bop", [128, DCH], F32)
        sb_kvnw = SB("sb_kvnw", [128, KLC], F32)
        sb_swap64 = SB("sb_swap64", [128, 128], BF16)
        sb_swap128 = SB("sb_swap128", [128, 128], BF16)
        sb_mrow = SB("sb_mrow", [1, 64], F32)
        sb_mtil = SB("sb_mtil", [128, H + GNQ], F32)
        arenaA = SB("arenaA", [128, 27 * 1024 - 256], BF16)
        arenaB = SB("arenaB", [128, 22016], BF16)
        rowsb2 = SB("rowsb2", [1, 512], BF16)
        rows2q = SB("rows2q", [128, 1024], F32)
        rows = SB("rows", [1, 1536], F32)
        rowsb = SB("rowsb", [1, 1024], BF16)

        sb_const = SB("sb_const", [1, 8], F32)
        pbt = [es.enter_context(nc.psum_tensor("pb%d" % i, [128, 512], F32))
               for i in range(8)]
        pb = [t[:, :] for t in pbt]

        def pbvr_const(bank, r0, r1):
            return pbt[bank][r0:r1, :]

        class Chain:
            def __init__(self, cs, hs, ss_, js=None):
                self.cs, self.hs, self.ss, self.js = cs, hs, ss_, js
                self.nc_, self.nh, self.ns, self.nj = 0, 0, 0, 0
                self.ops = []

            def add(self, eng, fn, kind="c", n=1, wc=None, wh=None, ws=None,
                    wj=None):
                c0 = self.nc_ if wc is None else wc
                h0 = self.nh if wh is None else wh
                s0 = self.ns if ws is None else ws
                j0 = self.nj if wj is None else wj
                self.ops.append((eng, fn, kind, c0, h0, s0, j0,
                                 kind != "c" and n > 1))
                if kind == "c":
                    self.nc_ += n
                elif kind == "h":
                    self.nh += 16 * n
                elif kind == "j":
                    self.nj += 16 * n
                else:
                    self.ns += 16 * n

            def emit(self):
                for eng in ENGS:
                    sub = [o for o in self.ops if o[0] == eng]
                    if not sub:
                        continue

                    def body(e, sub=sub):
                        for _, fn, kind, c0, h0, s0, j0, selfinc in sub:
                            if c0:
                                e.wait_ge(self.cs, c0)
                            if h0:
                                e.wait_ge(self.hs, h0)
                            if s0:
                                e.wait_ge(self.ss, s0)
                            if j0:
                                e.wait_ge(self.js, j0)
                            last = fn(e)
                            if selfinc:
                                continue
                            if kind == "c":
                                last.then_inc(self.cs, 1)
                            elif kind == "h":
                                last.then_inc(self.hs, 16)
                            elif kind == "j":
                                last.then_inc(self.js, 16)
                            else:
                                last.then_inc(self.ss, 16)
                    getattr(blk, eng)(body)
                for eng in ENGS:
                    def tail(e):
                        if self.nc_:
                            e.wait_ge(self.cs, self.nc_)
                        if self.nh:
                            e.wait_ge(self.hs, self.nh)
                        if self.ns:
                            e.wait_ge(self.ss, self.ns)
                        if self.nj:
                            e.wait_ge(self.js, self.nj)
                    getattr(blk, eng)(tail)

        class LoopChain:
            def __init__(self, cs, dma_sems):
                self.cs = cs
                self.dma = dict(dma_sems)
                self.ops = []
                self.cnt = {"c": 0}
                for k in self.dma:
                    self.cnt[k] = 0

            def add(self, eng, fn, kind="c"):
                self.ops.append((eng, fn, kind, dict(self.cnt)))
                self.cnt[kind] += 1 if kind == "c" else 16

            def emit(self, nidx, base=None, extra_entry=None):
                per = dict(self.cnt)
                for eng in ENGS:
                    sub = [o for o in self.ops if o[0] == eng]
                    if not sub:
                        continue

                    def body(e, sub=sub, per=per):
                        n = nv(e, nidx) if isinstance(nidx, int) else None
                        nn = n if n is not None else nidx
                        bexpr = base(e) if base is not None else None
                        if extra_entry is not None:
                            extra_entry(e, bexpr)
                        with e.Fori(0, nn) as i:
                            ib = (i + bexpr) if bexpr is not None else i
                            for _, fn, kind, snap in sub:
                                if snap["c"]:
                                    e.wait_ge(self.cs, ib * per["c"] + snap["c"])
                                else:
                                    with e.If(ib > 0):
                                        e.wait_ge(self.cs, ib * per["c"])
                                for k, sem in self.dma.items():
                                    if snap[k]:
                                        e.wait_ge(sem, ib * per[k] + snap[k])
                                last = fn(e, i, ib)
                                if kind == "c":
                                    last.then_inc(self.cs, 1)
                                else:
                                    last.then_inc(self.dma[kind], 16)
                        nend = (nn + bexpr) if bexpr is not None else nn
                        if per["c"]:
                            e.wait_ge(self.cs, nend * per["c"])
                        for k, sem in self.dma.items():
                            if per[k]:
                                e.wait_ge(sem, nend * per[k])
                    getattr(blk, eng)(body)

        semnames = ["bar"]
        for grp in "abcdefgnzquwo":
            for i in range(6):
                semnames.append("%s%d" % (grp, i))
        S = {nm: SEM(nm) for nm in semnames}

        CONSTS = [EPS_LN, EPS_RMS, EPS_FFN, 1.0]
        for ci, cv in enumerate(CONSTS):
            nc.const_aps.register(F32, cv, sb_const[0:1, ci:ci + 1])

        blk = es.enter_context(nc.Block("prog"))
        ENGS = ["sync", "gpsimd", "scalar", "vector", "tensor"]
        barn = [0]

        def barrier():
            barn[0] += 1
            tgt = 5 * barn[0]
            for en in ENGS:
                def f(e, tgt=tgt):
                    e.nop().then_inc(S["bar"], 1)
                    e.wait_ge(S["bar"], tgt)
                getattr(blk, en)(f)

        def nv(e, idx=0, lo=1, hi=16):
            return e.value_load(sb_meta[:1, idx:idx + 1], min_val=lo, max_val=hi)

        def chv(ap_, nch):
            return ap_.rearrange("p (c t) -> p c t", c=nch)

        def aA(off, size):
            return arenaA[:, off:off + size]

        def aB(off, size):
            return arenaB[:, off:off + size]

        def xh(p):
            return arenaA[:, ts(p + 1, 8192)]

        def rawv(p):
            return arenaB[:, 2048:7168][:, ts(p, 2560)]

        def absbv(p):
            return arenaB[:, 0:2048][:, ts(p, 1024)]


        def p0():
            def f(e):
                e.dma_start(out=sb_meta[:], in_=G["meta"]).then_inc(S["u0"], 16)
                e.dma_start(out=sb_qnw[:], in_=G["q_norm_w"]).then_inc(S["u0"], 16)
                e.dma_start(out=sb_kvnw[:], in_=G["kv_norm_w"]).then_inc(S["u0"], 16)
                e.dma_start(out=sb_bqp[:], in_=G["bq_p"]).then_inc(S["u3"], 16)
                e.dma_start(out=sb_bkp[:], in_=G["bk_p"]).then_inc(S["u3"], 16)
                e.dma_start(out=sb_bop[:], in_=G["bo_p"]).then_inc(S["u3"], 16)
                e.dma_start(
                    out=sb_x[:].rearrange("p (c t) -> p c t", c=DCH),
                    in_=G["x_ext"].rearrange("(c p) t -> p c t", p=128),
                ).then_inc(S["u0"], 16)
                e.wait_ge(S["u0"], 64)
                e.wait_ge(S["u3"], 48)
            blk.sync(f)

            def f(e):
                e.dma_start(out=sb_swap64[:], in_=G["swap64"]).then_inc(S["u1"], 16)
                e.dma_start(out=sb_swap128[:], in_=G["swap128"]).then_inc(S["u1"], 16)
                for ci, cv in enumerate([EPS_LN, EPS_RMS, EPS_FFN, 1.0]):
                    e.memset(sb_const[0:1, ci:ci + 1], cv)
                e.memset(sb_ones[:], 1.0)
                e.memset(sb_onesr[:], 1.0)
                e.memset(sb_onesr32[:], 1.0)
                e.memset(sb_zrow[:], 0.0)
                e.memset(sb_mrow[:], 0.0).then_inc(S["u2"], 1)
                e.wait_ge(S["u1"], 32)
                e.wait_ge(S["u2"], 1)
            blk.gpsimd(f)

        DC1 = 31

        def p1():
            def ifw(e, cond_expr, sem, val):
                with e.If(cond_expr):
                    e.wait_ge(S[sem], val)

            def f(e):
                e.wait_ge(S["u0"], 64)
                e.wait_ge(S["u3"], 48)
                e.wait_ge(S["u1"], 32)
                e.wait_ge(S["u2"], 1)
                n1 = nv(e)
                with e.Fori(0, n1) as i:
                    p = i % 2
                    with e.If(i > 1):
                        e.wait_ge(S["a0"], DC1 * (i - 1))
                    with e.If(i > 0):
                        e.wait_ge(S["a0"], DC1 * (i - 1) + 9)
                    e.dma_start(
                        out=chv(arenaA[:, 0:8192], DCH),
                        in_=G["x_full"].rearrange("(c p) t -> p c t", p=128)[:, :, ts(i, TT)],
                    ).then_inc(S["a2"], 16)
                    with e.If(i > 0):
                        e.wait_ge(S["a0"], DC1 * i)
                    e.dma_start(out=aB(9728, 1024).bitcast(F32)[0:64, :],
                                in_=G["ycos"][:, ts(i, TT)]).then_inc(S["a3"], 16)
                    e.dma_start(out=aB(10752, 1024).bitcast(F32)[0:64, :],
                                in_=G["ysin"][:, ts(i, TT)]).then_inc(S["a3"], 16)
                    for c in range(5):
                        mw = 128 if c < 4 else 64
                        with e.If(i > 0):
                            e.wait_ge(S["a0"], DC1 * (i - 1) + 11 + 3 * c)
                        e.dma_start(
                            out=arenaB[:, 11776 + c * 2048:11776 + (c + 1) * 2048]
                            .rearrange("p (d m) -> p d m", d=16)[:, :, 0:mw],
                            in_=G["w_kva"].rearrange("(d p) m -> p d m", p=128)
                            [:, :, c * 128:c * 128 + mw],
                        ).then_inc(S["a4"], 16)
                    e.wait_ge(S["a0"], DC1 * i + 8)
                    e.tensor_tensor(
                        chv(xh(p), DCH),
                        chv(arenaA[:, 0:8192], DCH),
                        absbv(p)[:, 0:512]
                        .unsqueeze(1).broadcast_to((128, DCH, 512)),
                        ALU.mult).then_inc(S["a0"], 1)
                    with e.If(i > 0):
                        e.wait_ge(S["a0"], DC1 * (i - 1) + 25)
                    for c in range(4):
                        e.wait_ge(S["a0"], DC1 * i + 12 + 3 * c)
                        e.tensor_tensor(
                            aB(7168 + c * 512, 512),
                            rawv(p)[:, c * 512:(c + 1) * 512],
                            rawv(p)[:, c * 512:(c + 1) * 512],
                            ALU.mult).then_inc(S["a0"], 1)
            blk.gpsimd(f)

            def f(e):
                n1 = nv(e)
                with e.Fori(0, n1) as i:
                    p = i % 2
                    with e.If(i > 1):
                        e.wait_ge(S["a0"], DC1 * (i - 1))
                    e.wait_ge(S["a2"], 16 * i + 16)
                    e.activation(xh(p),
                                 arenaA[:, 0:8192], AF.Square).then_inc(S["a0"], 1)
                    e.wait_ge(S["a0"], DC1 * i + 3)
                    e.activation(rows[:1, 1024:1536],
                                 rows[:1, 512:1024],
                                 AF.Ln, bias=EPS_LN).then_inc(S["a0"], 1)
                    e.wait_ge(S["a0"], DC1 * i + 4)
                    e.activation(rows[:1, 1024:1536],
                                 rows[:1, 1024:1536],
                                 AF.Exp, scale=-0.5).then_inc(S["a0"], 1)
                    e.wait_ge(S["a0"], DC1 * i + 7)
                    e.copy(absbv(p)[:, 0:512], pb[2])
                    e.copy(absbv(p)[:, 512:1024], pb[4]) \
                        .then_inc(S["a0"], 1)
                    for c in range(5):
                        e.wait_ge(S["a0"], DC1 * i + 11 + 3 * c)
                        with e.If(i > 0):
                            e.wait_ge(S["a1"], 80 * i)
                        mw = 128 if c < 4 else 64
                        e.copy(rawv(p)[:, c * 512:(c + 1) * 512][0:mw, :],
                               pb[6 + (c % 2)][0:mw, :]).then_inc(S["a0"], 1)
                    e.wait_ge(S["a0"], DC1 * i + 25)
                    e.activation(rows[:1, 512:1024],
                                 pb[2][0:1, :], AF.Ln, bias=EPS_RMS,
                                 scale=1.0 / KL).then_inc(S["a0"], 1)
                    e.wait_ge(S["a0"], DC1 * i + 26)
                    e.activation(rows[:1, 512:1024],
                                 rows[:1, 512:1024],
                                 AF.Exp, scale=-0.5).then_inc(S["a0"], 1)
                    e.wait_ge(S["a0"], DC1 * i + 27)
                    with e.If(i > 0):
                        e.wait_ge(S["a0"], DC1 * (i - 1) + 29)
                    e.copy(rowsb2[:1, 0:512],
                           rows[:1, 512:1024]).then_inc(S["a0"], 1)
            blk.scalar(f)

            def f(e):
                n1 = nv(e)
                with e.Fori(0, n1) as i:
                    p = i % 2
                    with e.If(i > 1):
                        e.wait_ge(S["a0"], DC1 * (i - 1))
                    e.wait_ge(S["a0"], DC1 * i + 2)
                    with e.If(i > 0):
                        e.wait_ge(S["a0"], DC1 * (i - 1) + 28)
                    e.tensor_scalar_mul(rows[:1, 0:512],
                                        pb[0][0:1, :], 1.0 / D)
                    e.tensor_tensor(rows[:1, 512:1024],
                                    rows[:1, 0:512],
                                    rows[:1, 0:512], ALU.mult)
                    e.scalar_tensor_tensor(rows[:1, 512:1024],
                                           pb[0][32:33, :], 1.0 / D,
                                           rows[:1, 512:1024],
                                           ALU.mult, ALU.subtract).then_inc(S["a0"], 1)
                    e.wait_ge(S["a0"], DC1 * i + 5)
                    e.tensor_copy(rowsb[:1, 0:512],
                                  rows[:1, 1024:1536])
                    e.tensor_tensor(rows[:1, 0:512],
                                    rows[:1, 0:512],
                                    rows[:1, 1024:1536], ALU.mult)
                    e.tensor_scalar_mul(rowsb[:1, 512:1024],
                                        rows[:1, 0:512],
                                        -1.0).then_inc(S["a0"], 1)
                    e.wait_ge(S["a0"], DC1 * i + 9)
                    e.tensor_tensor(
                        chv(xh(p), DCH),
                        chv(xh(p), DCH),
                        absbv(p)[:, 512:1024]
                        .unsqueeze(1).broadcast_to((128, DCH, 512)),
                        ALU.add).then_inc(S["a0"], 1)
                    e.wait_ge(S["a0"], DC1 * i + 29)
                    with e.If(i > 0):
                        e.wait_ge(S["a1"], 80 * i)
                    for c in range(4):
                        e.scalar_tensor_tensor(
                            aA(24576 + c * 512, 512),
                            rawv(p)[:, c * 512:(c + 1) * 512],
                            sb_kvnw[:, c:c + 1],
                            pb[2][:, :],
                            ALU.mult, ALU.mult)
                    e.nop().then_inc(S["a0"], 1)
                    e.wait_ge(S["a3"], 32 * i + 32)
                    e.tensor_tensor(aA(24576 + 4 * 512, 512)[0:64, :],
                                    rawv(p)[:, 4 * 512:5 * 512][0:64, :],
                                    aB(9728, 1024).bitcast(F32)[0:64, 0:512],
                                    ALU.mult)
                    e.tensor_tensor(aB(9216, 512)[0:64, :],
                                    pb[4][0:64, :],
                                    aB(10752, 1024).bitcast(F32)[0:64, 0:512],
                                    ALU.mult)
                    e.tensor_tensor(aA(24576 + 4 * 512, 512)[0:64, :],
                                    aA(24576 + 4 * 512, 512)[0:64, :],
                                    aB(9216, 512)[0:64, :], ALU.add).then_inc(S["a0"], 1)
            blk.vector(f)

            def f(e):
                e.wait_ge(S["u1"], 32)
                e.wait_ge(S["u2"], 1)
                n1 = nv(e)
                with e.Fori(0, n1) as i:
                    p = i % 2
                    with e.If(i > 1):
                        e.wait_ge(S["a0"], DC1 * (i - 1))
                    e.wait_ge(S["a2"], 16 * i + 16)
                    e.wait_ge(S["a0"], DC1 * i + 1)
                    with e.If(i > 0):
                        e.wait_ge(S["a0"], DC1 * (i - 1) + 3)
                    for d in range(DCH):
                        e.matmul(pb[0][0:1, :], sb_ones[:],
                                 arenaA[:, d * 512:(d + 1) * 512],
                                 start=(d == 0), stop=(d == DCH - 1))
                    mm = None
                    for d in range(DCH):
                        mm = e.matmul(pb[0][32:33, :], sb_ones[:],
                                      xh(p)[:, d * 512:(d + 1) * 512],
                                      start=(d == 0), stop=(d == DCH - 1),
                                      tile_position=(0, 32))
                    mm.then_inc(S["a0"], 1)
                    e.wait_ge(S["a0"], DC1 * i + 6)
                    with e.If(i > 0):
                        e.wait_ge(S["a0"], DC1 * i)
                    e.matmul(pb[2][:, :], sb_onesr[:],
                             rowsb[:1, 0:512], start=True, stop=True)
                    e.matmul(pb[4][:, :], sb_onesr[:],
                             rowsb[:1, 512:1024],
                             start=True, stop=True).then_inc(S["a0"], 1)
                    e.wait_ge(S["a0"], DC1 * i + 10)
                    for c in range(5):
                        e.wait_ge(S["a4"], 80 * i + 16 * (c + 1))
                        if c < 2:
                            with e.If(i > 0):
                                e.wait_ge(S["a0"], DC1 * (i - 1) + 12 + 3 * 3 + 3 * c)
                        else:
                            e.wait_ge(S["a0"], DC1 * i + 12 + 3 * (c - 2))
                        mw = 128 if c < 4 else 64
                        mm = None
                        for d in range(DCH):
                            mm = e.matmul(
                                pb[6 + (c % 2)][0:mw, :],
                                arenaB[:, 11776 + c * 2048 + d * 128:
                                       11776 + c * 2048 + d * 128 + mw],
                                xh(p)[:, d * 512:(d + 1) * 512],
                                start=(d == 0), stop=(d == DCH - 1))
                        mm.then_inc(S["a0"], 1)
                    e.wait_ge(S["a0"], DC1 * i + 24)
                    mm = None
                    for c in range(4):
                        mm = e.matmul(pb[2][0:1, :], sb_ones[:],
                                      aB(7168 + c * 512, 512),
                                      start=(c == 0), stop=(c == 3))
                    e.matmul(pb[4][0:64, :], sb_swap64[0:64, 0:64],
                             rawv(p)[:, 4 * 512:5 * 512][0:64, :],
                             start=True, stop=True).then_inc(S["a0"], 1)
                    e.wait_ge(S["a0"], DC1 * i + 28)
                    e.matmul(pb[2][:, :], sb_onesr[:],
                             rowsb2[:1, 0:512],
                             start=True, stop=True).then_inc(S["a0"], 1)
            blk.tensor(f)

            def f(e):
                n1 = nv(e)
                with e.Fori(0, n1) as i:
                    p = i % 2
                    e.wait_ge(S["a0"], DC1 * i + 31)
                    for c in range(4):
                        e.dma_start(
                            out=latscr[c][:, ts(i, TT)],
                            in_=aA(24576 + c * 512, 512),
                        ).then_inc(S["a1"], 16)
                    e.dma_start(
                        out=latscr[4][0:64, ts(i, TT)],
                        in_=aA(24576 + 4 * 512, 512)[0:64, :],
                    ).then_inc(S["a1"], 16)
                n1b = nv(e)
                e.wait_ge(S["a1"], n1b * 80)
            blk.sync(f)


        def p2():
            ch = Chain(S["b0"], S["b1"], S["b2"])
            XB = aA(0, 8192)
            X2 = aA(8192, 8192)
            QRAW = aB(0, 6144)
            QLN = aB(6144, 6144)
            WTI = aB(12288, 2048)
            QOUT = aB(14336, 512)
            ROPEB = aB(14848, 512)
            Q2 = aA(16384, 512)
            COSE = aB(15360, 1024).bitcast(F32)
            SINE = aB(16384, 1024).bitcast(F32)
            T1R = aA(16896, 512)
            for tt in range(NTE):
                ch.add("gpsimd", lambda e, tt=tt: e.tensor_copy(
                    chv(XB, DCH),
                    sb_x[:].rearrange("p (c t) -> p c t", c=DCH)[:, :, tt * TT:(tt + 1) * TT]))
                ch.add("scalar", lambda e: e.activation(X2, XB, AF.Square))
                def stats(e):
                    for d in range(DCH):
                        e.matmul(pb[0][0:1, :], sb_ones[:], XB[:, d * 512:(d + 1) * 512],
                                 start=(d == 0), stop=(d == DCH - 1))
                    mm = None
                    for d in range(DCH):
                        mm = e.matmul(pb[0][32:33, :], sb_ones[:],
                                      X2[:, d * 512:(d + 1) * 512],
                                      start=(d == 0), stop=(d == DCH - 1),
                                      tile_position=(0, 32))
                    return mm
                ch.add("tensor", stats)
                def mv(e):
                    e.tensor_scalar_mul(rows[:1, 0:512], pb[0][0:1, :], 1.0 / D)
                    e.tensor_tensor(rows[:1, 512:1024], rows[:1, 0:512],
                                    rows[:1, 0:512], ALU.mult)
                    return e.scalar_tensor_tensor(rows[:1, 512:1024], pb[0][32:33, :],
                                                  1.0 / D, rows[:1, 512:1024],
                                                  ALU.mult, ALU.subtract)
                ch.add("vector", mv)
                ch.add("scalar", lambda e: e.activation(rows[:1, 1024:1536],
                                                        rows[:1, 512:1024],
                                                        AF.Ln, bias=EPS_LN))
                ch.add("scalar", lambda e: e.activation(rows[:1, 1024:1536],
                                                        rows[:1, 1024:1536],
                                                        AF.Exp, scale=-0.5))
                def ab_(e):
                    e.tensor_tensor(rows[:1, 0:512], rows[:1, 0:512],
                                    rows[:1, 1024:1536], ALU.mult)
                    return e.tensor_scalar_mul(rows[:1, 0:512], rows[:1, 0:512], -1.0)
                ch.add("vector", ab_)
                def bc(e):
                    e.matmul(pb[2][:, :], sb_onesr32[:], rows[:1, 1024:1536],
                             start=True, stop=True)
                    return e.matmul(pb[3][:, :], sb_onesr32[:], rows[:1, 0:512],
                                    start=True, stop=True)
                ch.add("tensor", bc)
                def cp(e):
                    e.copy(rows2q[:, 0:512], pb[2])
                    return e.copy(rows2q[:, 512:1024], pb[3])
                ch.add("scalar", cp)
                ch.add("gpsimd", lambda e: e.tensor_tensor(
                    chv(X2, DCH), chv(XB, DCH),
                    rows2q[:, 0:512].unsqueeze(1).broadcast_to((128, DCH, 512)),
                    ALU.mult))
                ch.add("vector", lambda e: e.tensor_tensor(
                    chv(X2, DCH), chv(X2, DCH),
                    rows2q[:, 512:1024].unsqueeze(1).broadcast_to((128, DCH, 512)),
                    ALU.add))
                for c in range(QLC):
                    ch.add("gpsimd", lambda e, c=c: e.dma_start(
                        out=WTI[:].rearrange("p (d m) -> p d m", d=16),
                        in_=G["w_qa"].rearrange("(d p) m -> p d m", p=128)
                        [:, :, c * 128:(c + 1) * 128]), kind="s")
                    def qmm(e, c=c):
                        mm = None
                        for d in range(DCH):
                            mm = e.matmul(pb[6][:, :],
                                          WTI[:, d * 128:(d + 1) * 128],
                                          X2[:, d * 512:(d + 1) * 512],
                                          start=(d == 0), stop=(d == DCH - 1))
                        return mm
                    ch.add("tensor", qmm)
                    ch.add("scalar", lambda e, c=c: e.copy(
                        QRAW[:, c * 512:(c + 1) * 512], pb[6]))
                    ch.add("gpsimd", lambda e, c=c: e.tensor_tensor(
                        Q2, QRAW[:, c * 512:(c + 1) * 512],
                        QRAW[:, c * 512:(c + 1) * 512], ALU.mult))
                    ch.add("tensor", lambda e, c=c: e.matmul(
                        pb[5][0:1, :], sb_ones[:], Q2,
                        start=(c == 0), stop=(c == QLC - 1)))
                ch.add("scalar", lambda e: e.activation(
                    rows[:1, 512:1024], pb[5][0:1, :], AF.Ln,
                    bias=EPS_RMS, scale=1.0 / QL))
                ch.add("scalar", lambda e: e.activation(
                    rows[:1, 512:1024], rows[:1, 512:1024], AF.Exp, scale=-0.5))
                ch.add("scalar", lambda e: e.copy(rowsb2[:1, 0:512], rows[:1, 512:1024]))
                ch.add("tensor", lambda e: e.matmul(
                    pb[4][:, :], sb_onesr[:], rowsb2[:1, 0:512], start=True, stop=True))
                def qln(e):
                    last = None
                    for c in range(QLC):
                        last = e.scalar_tensor_tensor(
                            QLN[:, c * 512:(c + 1) * 512],
                            QRAW[:, c * 512:(c + 1) * 512],
                            sb_qnw[:, c:c + 1], pb[4][:, :], ALU.mult, ALU.mult)
                    return last
                ch.add("vector", qln)
                ch.add("gpsimd", lambda e, tt=tt: e.dma_start(
                    out=COSE[:, 0:512], in_=G["ycos_e"][:, tt * TT:(tt + 1) * TT]), kind="s")
                ch.add("gpsimd", lambda e, tt=tt: e.dma_start(
                    out=SINE[:, 0:512], in_=G["ysin_e"][:, tt * TT:(tt + 1) * TT]), kind="s")
                for r in range(8):
                    ms = [2 * r, 2 * r + 1, 16 + r]
                    for m in ms:
                        ch.add("gpsimd", lambda e, m=m: e.dma_start(
                            out=WTI[:, 0:1536].rearrange("p (d m2) -> p d m2", d=12),
                            in_=G["w_qb_p"].rearrange("(d p) m2 -> p d m2", p=128)
                            [:, :, m * 128:(m + 1) * 128]), kind="s")
                        def pmm(e, m=m):
                            mm = None
                            for c in range(QLC):
                                mm = e.matmul(pb[6][:, :],
                                              WTI[:, c * 128:(c + 1) * 128],
                                              QLN[:, c * 512:(c + 1) * 512],
                                              start=(c == 0), stop=(c == QLC - 1))
                            return mm
                        ch.add("tensor", pmm)
                        if m < 16:
                            ch.add("scalar", lambda e: e.copy(QOUT, pb[6]))
                            ch.add("gpsimd", lambda e: e.tensor_tensor(
                                Q2, QOUT, QOUT, ALU.mult))
                            ch.add("tensor", lambda e, m=m: e.matmul(
                                pbvr_const(5, 32 * (m % 2), 32 * (m % 2) + 1),
                                sb_ones[:], Q2, start=True, stop=True,
                                tile_position=(0, 32 * (m % 2))))
                        else:
                            ch.add("scalar", lambda e: e.copy(ROPEB, pb[6]))
                            ch.add("tensor", lambda e: e.matmul(
                                pb[1][:, :], sb_swap64[:, :], ROPEB,
                                start=True, stop=True))
                            def comb(e):
                                e.tensor_tensor(T1R, ROPEB, COSE[:, 0:512], ALU.mult)
                                e.tensor_tensor(QOUT, pb[1][:, :], SINE[:, 0:512],
                                                ALU.mult)
                                return e.tensor_tensor(QOUT, QOUT, T1R, ALU.add)
                            ch.add("vector", comb)
                            ch.add("gpsimd", lambda e: e.tensor_tensor(
                                Q2, QOUT, QOUT, ALU.mult))
                            def ssrope(e):
                                e.matmul(pbvr_const(4, 0, 1),
                                         sb_ones[0:64, :], Q2[0:64, :],
                                         start=True, stop=True,
                                         tile_position=(0, 0))
                                return e.matmul(pbvr_const(4, 32, 33),
                                                sb_ones[64:128, :], Q2[64:128, :],
                                                start=True, stop=True,
                                                tile_position=(64, 32))
                            ch.add("tensor", ssrope)
                        ch.add("sync", lambda e, m=m, tt=tt: e.dma_start(
                            out=qscr[m][:, tt * TT:(tt + 1) * TT], in_=QOUT), kind="h")
                    for k in range(2):
                        def gsum(e, k=k):
                            e.tensor_copy(rows[:1, 0:512],
                                          pbt[5][32 * k:32 * k + 1, :])
                            e.tensor_tensor(rows[:1, 0:512], rows[:1, 0:512],
                                            pbt[4][32 * k:32 * k + 1, :],
                                            ALU.add)
                            return e.tensor_reduce(rows[:1, 1024 + k:1025 + k],
                                                   rows[:1, 0:512],
                                                   mybir.AxisListType.X, ALU.max)
                        ch.add("vector", gsum)
                    def gmax(e, r=r):
                        last = None
                        for k in range(2):
                            hh = 2 * r + k
                            last = e.tensor_tensor(sb_mrow[:1, hh:hh + 1],
                                                   sb_mrow[:1, hh:hh + 1],
                                                   rows[:1, 1024 + k:1025 + k],
                                                   ALU.max)
                        return last
                    ch.add("vector", gmax)
            ch.emit()


        def p5():
            lc5 = LoopChain(S["d0"], {"w": S["d1"], "l": S["d2"], "g": S["d3"]})
            LAT = aB(0, 2560)
            KWT = aB(2560, 8192)
            VWT = aB(10752, 8192)
            K2 = aA(0, 2048)
            KOUT = aA(2048, 2048)
            VOUT = aA(4096, 2048)
            lc5.add("sync", lambda e, i, ib: e.dma_start(
                out=LAT[:, 0:2048].rearrange("p (c t) -> p c t", c=4),
                in_=latscr[0:4, :, ts(i, TT)].rearrange("c p t -> p c t")), kind="l")
            lc5.add("sync", lambda e, i, ib: e.dma_start(
                out=LAT[0:64, 2048:2560],
                in_=latscr[4][0:64, ts(i, TT)]), kind="l")
            lc5.add("gpsimd", lambda e, i, ib: e.dma_start(
                out=KWT[:].rearrange("p (c m) -> p c m", c=4),
                in_=G["w_kvb_p"].rearrange("(c p) m -> p c m", p=128)[:, :, 0:2048]),
                kind="g")
            lc5.add("gpsimd", lambda e, i, ib: e.dma_start(
                out=VWT[:].rearrange("p (c m) -> p c m", c=4),
                in_=G["w_kvb_p"].rearrange("(c p) m -> p c m", p=128)[:, :, 2048:4096]),
                kind="g")
            for hg in range(4):
                def kM(e, i, ib, hg=hg):
                    mm = None
                    for q in range(4):
                        for lc in range(4):
                            mm = e.matmul(
                                pb[4 + q][:, :],
                                KWT[:, lc * 2048 + (4 * hg + q) * 128:
                                    lc * 2048 + (4 * hg + q) * 128 + 128],
                                LAT[:, lc * 512:(lc + 1) * 512],
                                start=(lc == 0), stop=(lc == 3))
                    return mm
                lc5.add("tensor", kM)
                def kE(e, i, ib):
                    last = None
                    for q in range(4):
                        last = e.copy(KOUT[:, q * 512:(q + 1) * 512], pb[4 + q])
                    return last
                lc5.add("scalar", kE)
                lc5.add("gpsimd", lambda e, i, ib: e.tensor_tensor(
                    K2, KOUT, KOUT, ALU.mult))
                def kS(e, i, ib, hg=hg):
                    mm = None
                    for q in range(4):
                        mm = e.matmul(pb[0][0:1, :], sb_ones[:],
                                      K2[:, q * 512:(q + 1) * 512],
                                      start=(hg == 0 and q == 0), stop=False)
                    return mm
                lc5.add("tensor", kS)
                lc5.add("sync", lambda e, i, ib, hg=hg: e.dma_start(
                    out=kscr[4 * hg:4 * hg + 4, :, ts(i, TT)].rearrange("c p t -> p c t"),
                    in_=KOUT[:].rearrange("p (c t) -> p c t", c=4)), kind="w")
            lc5.add("gpsimd", lambda e, i, ib: e.tensor_tensor(
                K2[0:64, 0:512], LAT[0:64, 4 * 512:5 * 512],
                LAT[0:64, 4 * 512:5 * 512], ALU.mult))
            lc5.add("tensor", lambda e, i, ib: e.matmul(
                pb[0][0:1, :], sb_ones[0:64, :], K2[0:64, 0:512],
                start=False, stop=True))
            lc5.add("vector", lambda e, i, ib: e.tensor_reduce(
                rows[:1, 1024:1025], pb[0][0:1, :],
                mybir.AxisListType.X, ALU.max))
            lc5.add("vector", lambda e, i, ib: e.tensor_tensor(
                sb_mrow[:1, 16:17], sb_mrow[:1, 16:17],
                rows[:1, 1024:1025], ALU.max))
            for sub in range(4):
                def vM(e, i, ib, sub=sub):
                    mm = None
                    for vg in range(4):
                        for lc in range(4):
                            mm = e.matmul(
                                pb[4 + vg][:, :],
                                LAT[:, lc * 512 + sub * 128:lc * 512 + sub * 128 + 128],
                                VWT[:, lc * 2048 + vg * 512:lc * 2048 + (vg + 1) * 512],
                                start=(lc == 0), stop=(lc == 3))
                    return mm
                lc5.add("tensor", vM)
                def vE(e, i, ib):
                    last = None
                    for q in range(4):
                        last = e.copy(VOUT[:, q * 512:(q + 1) * 512], pb[4 + q])
                    return last
                lc5.add("scalar", vE)
                lc5.add("sync", lambda e, i, ib, sub=sub: e.dma_start(
                    out=vscr[:, :, :].rearrange("a p m -> a p m")[ts(4 * i + sub, 1)]
                    .squeeze(0),
                    in_=VOUT), kind="w")
            lc5.emit(0)

        def pmfin():
            ch = Chain(S["e0"], S["e1"], S["e2"])
            ch.add("scalar", lambda e: e.activation(rows[:1, 0:32], sb_mrow[:1, 0:32],
                                                    AF.Sqrt))
            ch.add("vector", lambda e: e.tensor_scalar(
                rows[:1, 0:16], rows[:1, 0:16],
                rows[:1, 16:17], -SC192, ALU.mult, ALU.mult))
            ch.add("vector", lambda e: e.tensor_copy(rowsb[:1, 0:16],
                                                     rows[:1, 0:16]))
            ch.add("tensor", lambda e: e.matmul(pb[0][:, 0:16], sb_onesr[:],
                                                rowsb[:1, 0:16], start=True, stop=True))
            ch.add("scalar", lambda e: e.copy(sb_mtil[:, 0:16], pb[0][:, 0:16]))
            ch.emit()


        def p6():
            QN = aB(0, 512)
            QR = aB(512, 512)
            KT = aB(1024, 512)
            RT = aB(1536, 512)
            VT = aB(2048, 512)
            PT = aB(2560, 2048)
            MSK = aB(4608, 2048)
            RLB = aB(6656, 1024).bitcast(F32)

            def f(e):
                e.dma_start(out=MSK[:].rearrange("p (a t) -> p a t", a=4),
                            in_=G["mla_masks"].rearrange("(a p) t -> p a t", p=128)) \
                    .then_inc(S["e3"], 16)
                skb = nv(e, 4, lo=1, hi=48)
                cums = [nv(e, 5, lo=0, hi=48), nv(e, 6, lo=0, hi=48),
                        nv(e, 7, lo=0, hi=48)]
                for h in range(H):
                    for qt in range(NTE):
                        nhq = 3 * h + qt
                        bc = skb * h + cums[qt]
                        nkbm1 = nv(e, 1 + qt, lo=1, hi=16) - 1
                        pk = 5 * bc + (6 * nhq + 1) + 5 * nkbm1
                        e.wait_ge(S["f0"], pk + 4)
                        e.tensor_tensor(PT[:], PT[:], MSK[:], ALU.mult) \
                            .then_inc(S["f0"], 1)
            blk.gpsimd(f)

            def f(e):
                e.wait_ge(S["e3"], 16)
                skb = nv(e, 4, lo=1, hi=48)
                cums = [nv(e, 5, lo=0, hi=48), nv(e, 6, lo=0, hi=48),
                        nv(e, 7, lo=0, hi=48)]
                for h in range(H):
                    for qt in range(NTE):
                        nhq = 3 * h + qt
                        bc = skb * h + cums[qt]
                        nkbv = nv(e, 1 + qt, lo=1, hi=16)
                        with e.If(nhq > 0):
                            e.wait_ge(S["f0"], 5 * bc + 6 * nhq)
                        e.dma_start(out=QN[:], in_=qscr[h][:, qt * TT:(qt + 1) * TT]) \
                            .then_inc(S["f2"], 16)
                        e.dma_start(out=QR[0:64, :],
                                    in_=qscr[16 + h // 2]
                                    [64 * (h % 2):64 * (h % 2) + 64,
                                     qt * TT:(qt + 1) * TT]).then_inc(S["f2"], 16)
                        with e.Fori(0, nkbv) as i:
                            with e.If(i > 0):
                                e.wait_ge(S["f0"],
                                          5 * bc + (6 * nhq + 1) + 5 * i)
                            e.dma_start(out=KT[:],
                                        in_=kscr[h][:, ts(i, TT)]).then_inc(S["f1"], 16)
                            e.dma_start(out=RT[0:64, :],
                                        in_=latscr[4][0:64, ts(i, TT)]) \
                                .then_inc(S["f1"], 16)
                            e.dma_start(
                                out=VT[:].rearrange("p (a m) -> p a m", a=4),
                                in_=vscr[:, :, 128 * h:128 * h + 128]
                                [ts(i, 4)].rearrange("a p m -> p a m"),
                            ).then_inc(S["f1"], 16)
            blk.sync(f)

            def f(e):
                e.wait_ge(S["e3"], 16)
                skb = nv(e, 4, lo=1, hi=48)
                cums = [nv(e, 5, lo=0, hi=48), nv(e, 6, lo=0, hi=48),
                        nv(e, 7, lo=0, hi=48)]
                for h in range(H):
                    for qt in range(NTE):
                        nhq = 3 * h + qt
                        bc = skb * h + cums[qt]
                        base = 6 * nhq
                        nkbv = nv(e, 1 + qt, lo=1, hi=16)
                        nkbm1 = nkbv - 1
                        with e.If(nhq + 0 > 0):
                            e.wait_ge(S["f0"], 5 * bc + base)
                        e.matmul(pb[6][:, :], sb_zrow[:1, :], rowsb[:1, 0:512],
                                 start=True, stop=False)
                        e.matmul(pb[7][0:1, :], sb_zrow[:1, 0:1], rowsb[:1, 0:512],
                                 start=True, stop=False).then_inc(S["f0"], 1)

                        def smx(lo, cidx, iexpr):
                            e.wait_ge(S["f0"], cidx)
                            e.wait_ge(S["f1"], 48 * (bc + iexpr) + 48)
                            e.wait_ge(S["f2"], 32 * nhq + 32)
                            mm = None
                            for sub in (0, 1) if lo else (2, 3):
                                bk = pb[sub % 4]
                                e.matmul(bk[:, :],
                                         KT[:, sub * 128:(sub + 1) * 128],
                                         QN[:], start=True, stop=False)
                                mm = e.matmul(bk[:, :],
                                              RT[0:64, sub * 128:(sub + 1) * 128],
                                              QR[0:64, :], start=False, stop=True)
                            mm.then_inc(S["f0"], 1)

                        def lav(cidx, fin=False):
                            e.wait_ge(S["f0"], cidx)
                            mm = None
                            for sub in range(4):
                                e.matmul(pb[7][0:1, :], sb_ones[:],
                                         PT[:, sub * 512:(sub + 1) * 512],
                                         start=False, stop=(fin and sub == 3))
                                mm = e.matmul(pb[6][:, :],
                                              VT[:, sub * 128:(sub + 1) * 128],
                                              PT[:, sub * 512:(sub + 1) * 512],
                                              start=False, stop=(fin and sub == 3))
                            mm.then_inc(S["f0"], 1)

                        with e.If(nkbm1 > 0):
                            with e.Fori(0, nkbm1) as i:
                                cb = 5 * bc + (base + 1) + 5 * i
                                smx(True, cb, i)
                                smx(False, cb + 2, i)
                                lav(cb + 4)
                        pk = 5 * bc + (base + 1) + 5 * nkbm1
                        smx(True, pk, nkbm1)
                        smx(False, pk + 2, nkbm1)
                        lav(pk + 5, fin=True)
                        e.wait_ge(S["f0"], pk + 7)
                        e.matmul(pb[5][:, :], sb_onesr[:], rowsb2[:1, 0:512],
                                 start=True, stop=True).then_inc(S["f0"], 1)
            blk.tensor(f)

            def f(e):
                e.wait_ge(S["e3"], 16)
                skb = nv(e, 4, lo=1, hi=48)
                cums = [nv(e, 5, lo=0, hi=48), nv(e, 6, lo=0, hi=48),
                        nv(e, 7, lo=0, hi=48)]
                for h in range(H):
                    for qt in range(NTE):
                        nhq = 3 * h + qt
                        bc = skb * h + cums[qt]
                        base = 6 * nhq
                        nkbv = nv(e, 1 + qt, lo=1, hi=16)
                        nkbm1 = nkbv - 1
                        with e.If(nkbm1 > 0):
                            with e.Fori(0, nkbm1) as i:
                                cb = 5 * bc + (base + 1) + 5 * i
                                e.wait_ge(S["f0"], cb + 1)
                                e.activation(PT[:, 0:512], pb[0], AF.Exp,
                                             bias=sb_mtil[:, h:h + 1], scale=SC192)
                                e.activation(PT[:, 512:1024], pb[1], AF.Exp,
                                             bias=sb_mtil[:, h:h + 1], scale=SC192) \
                                    .then_inc(S["f0"], 1)
                                e.wait_ge(S["f0"], cb + 3)
                                e.activation(PT[:, 1024:1536], pb[2], AF.Exp,
                                             bias=sb_mtil[:, h:h + 1], scale=SC192)
                                e.activation(PT[:, 1536:2048], pb[3], AF.Exp,
                                             bias=sb_mtil[:, h:h + 1], scale=SC192) \
                                    .then_inc(S["f0"], 1)
                        pk = 5 * bc + (base + 1) + 5 * nkbm1
                        e.wait_ge(S["f0"], pk + 1)
                        e.activation(PT[:, 0:512], pb[0], AF.Exp,
                                     bias=sb_mtil[:, h:h + 1], scale=SC192)
                        e.activation(PT[:, 512:1024], pb[1], AF.Exp,
                                     bias=sb_mtil[:, h:h + 1], scale=SC192) \
                            .then_inc(S["f0"], 1)
                        e.wait_ge(S["f0"], pk + 3)
                        e.activation(PT[:, 1024:1536], pb[2], AF.Exp,
                                     bias=sb_mtil[:, h:h + 1], scale=SC192)
                        e.activation(PT[:, 1536:2048], pb[3], AF.Exp,
                                     bias=sb_mtil[:, h:h + 1], scale=SC192) \
                            .then_inc(S["f0"], 1)
                        e.wait_ge(S["f0"], pk + 8)
                        e.copy(RLB[:, 0:512], pb[5]).then_inc(S["f0"], 1)
            blk.scalar(f)

            def f(e):
                e.wait_ge(S["e3"], 16)
                skb = nv(e, 4, lo=1, hi=48)
                cums = [nv(e, 5, lo=0, hi=48), nv(e, 6, lo=0, hi=48),
                        nv(e, 7, lo=0, hi=48)]
                for h in range(H):
                    for qt in range(NTE):
                        nhq = 3 * h + qt
                        bc = skb * h + cums[qt]
                        base = 6 * nhq
                        nkbm1 = nv(e, 1 + qt, lo=1, hi=16) - 1
                        pk = 5 * bc + (base + 1) + 5 * nkbm1
                        e.wait_ge(S["f0"], pk + 6)
                        e.reciprocal(rows[:1, 0:512], pb[7][0:1, :])
                        e.tensor_copy(rowsb2[:1, 0:512], rows[:1, 0:512]) \
                            .then_inc(S["f0"], 1)
                        e.wait_ge(S["f0"], pk + 9)
                        e.tensor_tensor(aA(1536 * h + 512 * qt, 512), pb[6][:, :],
                                        RLB[:, 0:512], ALU.mult).then_inc(S["f0"], 1)
            blk.vector(f)


        def p7():
            ch = Chain(S["g0"], S["g1"], S["g2"])
            WTI7 = aB(0, 2048)
            for dch in range(DCH):
                ch.add("gpsimd", lambda e, dch=dch: e.dma_start(
                    out=WTI7[:].rearrange("p (d m) -> p d m", d=16),
                    in_=G["w_o"].rearrange("(d p) m -> p d m", p=128)
                    [:, :, dch * 128:(dch + 1) * 128]), kind="s")
                def wmm(e, dch=dch):
                    mm = None
                    for t in range(NTE):
                        for hc in range(DCH):
                            mm = e.matmul(pb[t][:, :],
                                          WTI7[:, hc * 128:(hc + 1) * 128],
                                          aA(hc * 1536 + t * 512, 512),
                                          start=(hc == 0), stop=(hc == DCH - 1))
                    return mm
                ch.add("tensor", wmm)
                def res(e, dch=dch):
                    last = None
                    for t in range(NTE):
                        xsl = sb_x[:, dch * EXT + t * 512:dch * EXT + (t + 1) * 512]
                        last = e.tensor_tensor(xsl, xsl, pb[t][:, :], ALU.add)
                    return last
                ch.add("vector", res)
            ch.emit()

        def p8():
            ch = Chain(S["q0"], S["q1"], S["q2"])
            NGT = (2 * FH) // 128
            NPT = (D // 128) * 4
            for it in range(NGT + NPT):
                par = it % 2
                WQ = aB(par * 8192, 2048)
                TQ = aB(par * 8192 + 2048, 2048)
                B2 = aB(par * 8192 + 4096, 2048)
                SCB = aB(par * 8192 + 6144, 64)
                RSF = aB(par * 8192 + 6208, 128).bitcast(F32)
                if it < NGT:
                    src_ap = G["wg"][it * 128:(it + 1) * 128, :]
                    dst_ap = wgq[it * 128:(it + 1) * 128, :]
                else:
                    k = it - NGT
                    r, jj = k // 4, k % 4
                    src_ap = G["wp"][r * 128:(r + 1) * 128, jj * 2048:(jj + 1) * 2048]
                    dst_ap = wpq[r * 128:(r + 1) * 128, jj * 2048:(jj + 1) * 2048]
                wcov = max(0, ch.nc_ - 2) if it >= 2 else None
                ch.add("gpsimd", lambda e, s=src_ap, WQ=WQ: e.dma_start(out=WQ, in_=s),
                       kind="s", wc=wcov)
                eng = "vector" if it % 2 == 0 else "gpsimd"
                ch.add("scalar", lambda e, WQ=WQ, TQ=TQ: e.activation(
                    TQ[:], WQ[:], AF.Abs))
                ch.add("vector", lambda e, TQ=TQ, RSF=RSF: e.tensor_reduce(
                    RSF[:, 0:32], TQ[:].rearrange("p (g w) -> p g w", g=32),
                    mybir.AxisListType.X, ALU.add))
                ch.add("vector", lambda e, RSF=RSF: e.tensor_scalar(
                    RSF[:, 32:64], RSF[:, 0:32], 1.0 / 64, 1e-8,
                    ALU.mult, ALU.max))
                ch.add("vector", lambda e, SCB=SCB, RSF=RSF: e.tensor_copy(
                    SCB[:, 0:32], RSF[:, 32:64]))
                ch.add("vector", lambda e, SCB=SCB, RSF=RSF: e.reciprocal(
                    RSF[:, 32:64], SCB[:, 0:32]))
                def qmul(e, WQ=WQ, TQ=TQ, RSF=RSF):
                    return e.tensor_tensor(TQ[:].rearrange("p (g w) -> p g w", g=32),
                                           WQ[:].rearrange("p (g w) -> p g w", g=32),
                                           RSF[:, 32:64].unsqueeze(2)
                                           .broadcast_to((128, 32, 64)), ALU.mult)
                ch.add(eng, qmul)
                ch.add("vector", lambda e, TQ=TQ, B2=B2: e.tensor_single_scalar(
                    B2[:], TQ[:], -0.5, ALU.is_lt))
                ch.add("vector", lambda e, TQ=TQ, B2=B2: e.scalar_tensor_tensor(
                    TQ[:], TQ[:], 0.5, B2[:], ALU.is_gt, ALU.subtract))
                def qwh(e, WQ=WQ, TQ=TQ, SCB=SCB):
                    return e.tensor_tensor(WQ[:].rearrange("p (g w) -> p g w", g=32),
                                           TQ[:].rearrange("p (g w) -> p g w", g=32),
                                           SCB[:, 0:32].unsqueeze(2)
                                           .broadcast_to((128, 32, 64)), ALU.mult)
                ch.add(eng, qwh)
                ch.add("sync", lambda e, d=dst_ap, WQ=WQ: e.dma_start(out=d, in_=WQ),
                       kind="h")
            ch.emit()

        def emit_ffn(tiles, sems):
            ch = Chain(S[sems[0]], S[sems[1]], S[sems[2]], S[sems[3]])
            XB = aA(0, 8192)
            X2 = aA(8192, 8192)
            GLG = [aA(0, 4096), aA(4096, 4096)]
            GLU = [aA(16384, 4096), aA(20480, 4096)]
            HT = aB(0, 1024)
            H2Q = aB(1024, 1024)
            SG = aB(2048, 1024)
            WPT = [aB(3072 + k * 1024, 1024) for k in range(2)]
            INVB = aB(5120, 1024).bitcast(F32)
            for tt in tiles:
                ch.add("gpsimd", lambda e, tt=tt: e.tensor_copy(
                    chv(XB, DCH),
                    sb_x[:].rearrange("p (c t) -> p c t", c=DCH)[:, :, tt * TT:(tt + 1) * TT]))
                ch.add("scalar", lambda e: e.activation(X2, XB, AF.Square))
                def stats(e):
                    for d in range(DCH):
                        e.matmul(pb[0][0:1, :], sb_ones[:], XB[:, d * 512:(d + 1) * 512],
                                 start=(d == 0), stop=(d == DCH - 1))
                    mm = None
                    for d in range(DCH):
                        mm = e.matmul(pb[0][32:33, :], sb_ones[:],
                                      X2[:, d * 512:(d + 1) * 512],
                                      start=(d == 0), stop=(d == DCH - 1),
                                      tile_position=(0, 32))
                    return mm
                ch.add("tensor", stats)
                def mv(e):
                    e.tensor_scalar_mul(rows[:1, 0:512], pb[0][0:1, :], 1.0 / D)
                    e.tensor_tensor(rows[:1, 512:1024], rows[:1, 0:512],
                                    rows[:1, 0:512], ALU.mult)
                    return e.scalar_tensor_tensor(rows[:1, 512:1024], pb[0][32:33, :],
                                                  1.0 / D, rows[:1, 512:1024],
                                                  ALU.mult, ALU.subtract)
                ch.add("vector", mv)
                ch.add("scalar", lambda e: e.activation(rows[:1, 1024:1536],
                                                        rows[:1, 512:1024],
                                                        AF.Ln, bias=EPS_LN))
                ch.add("scalar", lambda e: e.activation(rows[:1, 1024:1536],
                                                        rows[:1, 1024:1536],
                                                        AF.Exp, scale=-0.5))
                def ab_(e):
                    e.tensor_tensor(rows[:1, 0:512], rows[:1, 0:512],
                                    rows[:1, 1024:1536], ALU.mult)
                    return e.tensor_scalar_mul(rows[:1, 0:512], rows[:1, 0:512], -1.0)
                ch.add("vector", ab_)
                def bc(e):
                    e.matmul(pb[1][:, :], sb_onesr32[:], rows[:1, 1024:1536],
                             start=True, stop=True)
                    return e.matmul(pb[2][:, :], sb_onesr32[:], rows[:1, 0:512],
                                    start=True, stop=True)
                ch.add("tensor", bc)
                def cp(e):
                    e.copy(rows2q[:, 0:512], pb[1])
                    return e.copy(rows2q[:, 512:1024], pb[2])
                ch.add("scalar", cp)
                ch.add("gpsimd", lambda e: e.tensor_tensor(
                    chv(X2, DCH), chv(XB, DCH),
                    rows2q[:, 0:512].unsqueeze(1).broadcast_to((128, DCH, 512)),
                    ALU.mult))
                ch.add("vector", lambda e: e.tensor_tensor(
                    chv(X2, DCH), chv(X2, DCH),
                    rows2q[:, 512:1024].unsqueeze(1).broadcast_to((128, DCH, 512)),
                    ALU.add))
                hcn, jcn, gmmc, hmulc = {}, {}, {}, {}

                def emit_loadpair(P, wc):
                    pp = P % 2
                    def wldg(e, P=P, pp=pp):
                        last = None
                        for d in range(DCH):
                            last = e.dma_start(
                                out=GLG[pp][:, d * 256:(d + 1) * 256],
                                in_=wgq[P * 256:(P + 1) * 256,
                                        d * 128:(d + 1) * 128],
                                transpose=True)
                            last.then_inc(S[sems[1]], 16)
                        return last
                    ch.add("sync", wldg, kind="h", n=16, wc=wc)
                    hcn[P] = ch.nh
                    def wldu(e, P=P, pp=pp):
                        last = None
                        for d in range(DCH):
                            last = e.dma_start(
                                out=GLU[pp][:, d * 256:(d + 1) * 256],
                                in_=wgq[FH + P * 256:FH + (P + 1) * 256,
                                        d * 128:(d + 1) * 128],
                                transpose=True)
                            last.then_inc(S[sems[3]], 16)
                        return last
                    ch.add("scalar", wldu, kind="j", n=16, wc=wc)
                    jcn[P] = ch.nj

                def emit_hss(jp2, fin):
                    def hss(e, jp2=jp2, fin=fin):
                        par2 = jp2 % 2
                        return e.matmul(pb[3][0:1, :], sb_ones[:],
                                        H2Q[:, par2 * 512:par2 * 512 + 512],
                                        start=(jp2 == 0), stop=fin)
                    ch.add("tensor", hss, wc=hmulc[jp2], wh=0, wj=0)

                emit_loadpair(0, None)
                emit_loadpair(1, None)
                for jp in range(64):
                    par, P, pp = jp % 2, jp // 2, (jp // 2) % 2
                    gb, ub = 4 + par, 6 + par
                    def gmm(e, par=par, pp=pp):
                        gb, ub = 4 + par, 6 + par
                        for d in range(DCH):
                            e.matmul(pb[gb][:, :],
                                     GLG[pp][:, d * 256 + par * 128:
                                             d * 256 + par * 128 + 128],
                                     X2[:, d * 512:(d + 1) * 512],
                                     start=(d == 0), stop=(d == DCH - 1))
                        mm = None
                        for d in range(DCH):
                            mm = e.matmul(pb[ub][:, :],
                                          GLU[pp][:, d * 256 + par * 128:
                                                  d * 256 + par * 128 + 128],
                                          X2[:, d * 512:(d + 1) * 512],
                                          start=(d == 0), stop=(d == DCH - 1))
                        return mm
                    ch.add("tensor", gmm, wc=hmulc.get(jp - 2),
                           wh=hcn[P], wj=jcn[P])
                    gmmc[jp] = ch.nc_
                    if par == 1 and jp <= 59:
                        emit_loadpair(P + 2, gmmc[jp])
                    ch.add("scalar", lambda e, par=par, gb=gb: e.activation(
                        SG[:, par * 512:par * 512 + 512], pb[gb], AF.Silu),
                        wh=0, wj=0)
                    if jp >= 2:
                        emit_hss(jp - 2, False)
                    def hmul2(e, par=par, ub=ub):
                        e.tensor_tensor(HT[:, par * 512:par * 512 + 512],
                                        SG[:, par * 512:par * 512 + 512],
                                        pb[ub], ALU.mult)
                        return e.tensor_tensor(
                            H2Q[:, par * 512:par * 512 + 512],
                            HT[:, par * 512:par * 512 + 512],
                            HT[:, par * 512:par * 512 + 512], ALU.mult)
                    ch.add("vector", hmul2, wh=0, wj=0)
                    hmulc[jp] = ch.nc_
                    ch.add("gpsimd", lambda e, jp=jp, par=par: e.dma_start(
                        out=hscr2[jp, :, :],
                        in_=HT[:, par * 512:par * 512 + 512]),
                        kind="s", wh=0, wj=0)
                emit_hss(62, False)
                emit_hss(63, True)
                ch.add("scalar", lambda e: e.activation(
                    rows[:1, 0:512], pb[3][0:1, :], AF.Ln, bias=EPS_FFN,
                    scale=1.0 / FH))
                ch.add("scalar", lambda e: e.activation(
                    rows[:1, 0:512], rows[:1, 0:512], AF.Exp, scale=-0.5))
                ch.add("scalar", lambda e: e.copy(rowsb2[:1, 0:512], rows[:1, 0:512]))
                ch.add("tensor", lambda e: e.matmul(
                    pb[3][:, :], sb_onesr[:], rowsb2[:1, 0:512], start=True, stop=True))
                ch.add("scalar", lambda e: e.copy(INVB[:, 0:512], pb[3]))
                for dh in range(2):
                    ybase = ch.nc_
                    ycnt, ymmc = {}, {}

                    def yld_emit(jc, dh=dh, ybase=ybase, ycnt=ycnt, ymmc=ymmc):
                        par = jc % 2
                        def both(e, jc=jc, dh=dh, par=par,
                                 sem=S[sems[1] if par == 0 else sems[3]]):
                            e.dma_start(
                                out=WPT[par][:],
                                in_=wpq[dh * 1024:(dh + 1) * 1024,
                                        jc * 128:(jc + 1) * 128],
                                transpose=True).then_inc(sem, 16)
                            return e.dma_start(
                                out=HT[:, par * 512:par * 512 + 512],
                                in_=hscr2[jc, :, :]).then_inc(sem, 16)
                        wcov = ymmc.get(jc - 2, ybase)
                        ch.add("sync" if par == 0 else "scalar", both,
                               kind="h" if par == 0 else "j", n=2, wc=wcov)
                        ycnt[jc] = (ch.nh, ch.nj)

                    yld_emit(0)
                    yld_emit(1)
                    for jc in range(64):
                        par = jc % 2
                        def ymm(e, jc=jc, dh=dh, par=par):
                            mm = None
                            for oc in range(8):
                                mm = e.matmul(
                                    pb[oc][:, :],
                                    WPT[par][:, oc * 128:(oc + 1) * 128],
                                    HT[:, par * 512:par * 512 + 512],
                                    start=(jc == 0), stop=(jc == 63))
                            return mm
                        ch.add("tensor", ymm, wh=ycnt[jc][0], wj=ycnt[jc][1])
                        ymmc[jc] = ch.nc_
                        if jc + 2 < 64:
                            yld_emit(jc + 2)
                    def yev(e, dh=dh, tt=tt):
                        last = None
                        for oc in range(8):
                            dchk = dh * 8 + oc
                            xsl = sb_x[:, dchk * EXT + tt * 512:
                                       dchk * EXT + (tt + 1) * 512]
                            e.tensor_tensor(rows2q[:, 0:512], pb[oc][:, :],
                                            INVB[:, 0:512], ALU.mult)
                            last = e.tensor_tensor(xsl, xsl, rows2q[:, 0:512],
                                                   ALU.add)
                        return last
                    ch.add("vector", yev, wh=0, wj=0)
            ch.emit()

        def p9():
            emit_ffn(list(range(NTE)), ("n0", "n1", "n2", "n3"))

        def p11():
            emit_ffn([1, 2], ("w0", "w1", "w2", "w3"))


        def p10():
            ch = Chain(S["z0"], S["z1"], S["z2"])
            XB = aA(0, 8192)
            H3 = aA(8192, 8192)
            WTI0 = aA(16384, 2048)
            WVT = aA(16384, 8192)
            QOUT = aA(24320, 512)
            ROPB = aA(24832, 512)
            COSB = aA(25344, 1024).bitcast(F32)
            SINB = aA(26368, 1024).bitcast(F32)
            PT = aA(0, 2048)
            RLB = aA(2048, 1024).bitcast(F32)
            def ogt(q2, qh):
                if q2 == 0:
                    return aA(4096 + qh * 512, 512)
                return aA(12288 + qh * 512, 512) if qh < 8 else \
                    aA(18432 + (qh - 8) * 512, 512)
            KST = aB(1024, 6144)
            VST = aB(7168, 6144)
            GMSK = aB(13312, 8192)
            BVB = rowsb2

            ch.add("gpsimd", lambda e: e.dma_start(
                out=GMSK[:].rearrange("p (a t) -> p a t", a=16),
                in_=G["gqa_masks"].rearrange("(a p) t -> p a t", p=128)), kind="s")
            ch.add("gpsimd", lambda e: e.dma_start(out=BVB[:1, :],
                                                   in_=G["bv_row"]), kind="s")
            for tt in range(NTE):
                ch.add("gpsimd", lambda e, tt=tt: e.tensor_copy(
                    chv(XB, DCH),
                    sb_x[:].rearrange("p (c t) -> p c t", c=DCH)[:, :, tt * TT:(tt + 1) * TT]))
                ch.add("scalar", lambda e: e.activation(H3, XB, AF.Square))
                def stats(e):
                    for d in range(DCH):
                        e.matmul(pb[0][0:1, :], sb_ones[:], XB[:, d * 512:(d + 1) * 512],
                                 start=(d == 0), stop=(d == DCH - 1))
                    mm = None
                    for d in range(DCH):
                        mm = e.matmul(pb[0][32:33, :], sb_ones[:],
                                      H3[:, d * 512:(d + 1) * 512],
                                      start=(d == 0), stop=(d == DCH - 1),
                                      tile_position=(0, 32))
                    return mm
                ch.add("tensor", stats)
                def mv(e):
                    e.tensor_scalar_mul(rows[:1, 0:512], pb[0][0:1, :], 1.0 / D)
                    e.tensor_tensor(rows[:1, 512:1024], rows[:1, 0:512],
                                    rows[:1, 0:512], ALU.mult)
                    return e.scalar_tensor_tensor(rows[:1, 512:1024], pb[0][32:33, :],
                                                  1.0 / D, rows[:1, 512:1024],
                                                  ALU.mult, ALU.subtract)
                ch.add("vector", mv)
                ch.add("scalar", lambda e: e.activation(rows[:1, 1024:1536],
                                                        rows[:1, 512:1024],
                                                        AF.Ln, bias=EPS_LN))
                ch.add("scalar", lambda e: e.activation(rows[:1, 1024:1536],
                                                        rows[:1, 1024:1536],
                                                        AF.Exp, scale=-0.5))
                def ab_(e):
                    e.tensor_tensor(rows[:1, 0:512], rows[:1, 0:512],
                                    rows[:1, 1024:1536], ALU.mult)
                    return e.tensor_scalar_mul(rows[:1, 0:512], rows[:1, 0:512], -1.0)
                ch.add("vector", ab_)
                def bc(e):
                    e.matmul(pb[1][:, :], sb_onesr32[:], rows[:1, 1024:1536],
                             start=True, stop=True)
                    return e.matmul(pb[2][:, :], sb_onesr32[:], rows[:1, 0:512],
                                    start=True, stop=True)
                ch.add("tensor", bc)
                def cp(e):
                    e.copy(rows2q[:, 0:512], pb[1])
                    return e.copy(rows2q[:, 512:1024], pb[2])
                ch.add("scalar", cp)
                ch.add("gpsimd", lambda e: e.tensor_tensor(
                    chv(H3, DCH), chv(XB, DCH),
                    rows2q[:, 0:512].unsqueeze(1).broadcast_to((128, DCH, 512)),
                    ALU.mult))
                ch.add("vector", lambda e: e.tensor_tensor(
                    chv(H3, DCH), chv(H3, DCH),
                    rows2q[:, 512:1024].unsqueeze(1).broadcast_to((128, DCH, 512)),
                    ALU.add))
                ch.add("gpsimd", lambda e, tt=tt: e.dma_start(
                    out=COSB[:, 0:512], in_=G["gcos_e"][:, tt * TT:(tt + 1) * TT]), kind="s")
                ch.add("gpsimd", lambda e, tt=tt: e.dma_start(
                    out=SINB[:, 0:512], in_=G["gsin_e"][:, tt * TT:(tt + 1) * TT]), kind="s")
                for g in range(GNKV):
                    ch.add("gpsimd", lambda e, g=g: e.dma_start(
                        out=WTI0[:].rearrange("p (d m) -> p d m", d=16),
                        in_=G["Wk_p"].rearrange("(d p) m -> p d m", p=128)
                        [:, :, g * 128:(g + 1) * 128]), kind="s")
                    def kmm(e):
                        mm = None
                        for d in range(DCH):
                            mm = e.matmul(pb[6][:, :], WTI0[:, d * 128:(d + 1) * 128],
                                          H3[:, d * 512:(d + 1) * 512],
                                          start=(d == 0), stop=(d == DCH - 1))
                        return mm
                    ch.add("tensor", kmm)
                    ch.add("vector", lambda e, g=g: e.tensor_scalar_add(
                        ROPB, pb[6], sb_bkp[:, g:g + 1]))
                    ch.add("tensor", lambda e: e.matmul(
                        pb[1][:, :], sb_swap128[:, :], ROPB, start=True, stop=True))
                    def kcomb(e, g=g, tt=tt):
                        e.tensor_tensor(rows2q[:, 0:512], ROPB, COSB[:, 0:512], ALU.mult)
                        e.tensor_tensor(rows2q[:, 512:1024], pb[1][:, :],
                                        SINB[:, 0:512], ALU.mult)
                        e.tensor_tensor(KST[:, g * 1536 + tt * 512:
                                            g * 1536 + (tt + 1) * 512],
                                        rows2q[:, 0:512], rows2q[:, 512:1024], ALU.add)
                        e.tensor_tensor(QOUT, KST[:, g * 1536 + tt * 512:
                                                  g * 1536 + (tt + 1) * 512],
                                        KST[:, g * 1536 + tt * 512:
                                            g * 1536 + (tt + 1) * 512], ALU.mult)
                        return e.nop()
                    ch.add("vector", kcomb)
                    ch.add("tensor", lambda e: e.matmul(
                        pb[3][0:1, :], sb_ones[:], QOUT, start=True, stop=True))
                    ch.add("vector", lambda e: e.tensor_reduce(
                        rows[:1, 1024:1025], pb[3][0:1, :],
                        mybir.AxisListType.X, ALU.max))
                    ch.add("vector", lambda e: e.tensor_tensor(
                        sb_mrow[:1, 48:49], sb_mrow[:1, 48:49],
                        rows[:1, 1024:1025], ALU.max))
                ch.add("gpsimd", lambda e: e.dma_start(
                    out=WVT[:].rearrange("p (d m) -> p d m", d=16),
                    in_=G["Wv"].rearrange("(d p) m -> p d m", p=128)), kind="s")
                def vmm(e, tt=tt):
                    mm = None
                    for sub in range(4):
                        for d in range(DCH):
                            e.matmul(pb[sub][:, :],
                                     H3[:, d * 512 + sub * 128:d * 512 + sub * 128 + 128],
                                     WVT[:, d * 512:(d + 1) * 512],
                                     start=(d == 0), stop=False)
                        mm = e.matmul(pb[sub][:, :], sb_onesr[:], BVB[:1, 0:512],
                                      start=False, stop=True)
                    return mm
                ch.add("tensor", vmm)
                def vev(e, tt=tt):
                    last = None
                    for sub in range(4):
                        last = e.copy(VST[:, (tt * 4 + sub) * 512:
                                          (tt * 4 + sub + 1) * 512], pb[sub])
                    return last
                ch.add("scalar", vev)
                if tt >= 1:
                    for qh in range(GNQ):
                        ch.add("gpsimd", lambda e, qh=qh: e.dma_start(
                            out=WTI0[:].rearrange("p (d m) -> p d m", d=16),
                            in_=G["Wq_p"].rearrange("(d p) m -> p d m", p=128)
                            [:, :, qh * 128:(qh + 1) * 128]), kind="s")
                        def qmm(e):
                            mm = None
                            for d in range(DCH):
                                mm = e.matmul(pb[6][:, :],
                                              WTI0[:, d * 128:(d + 1) * 128],
                                              H3[:, d * 512:(d + 1) * 512],
                                              start=(d == 0), stop=(d == DCH - 1))
                            return mm
                        ch.add("tensor", qmm)
                        ch.add("vector", lambda e, qh=qh: e.tensor_scalar_add(
                            ROPB, pb[6], sb_bqp[:, qh:qh + 1]))
                        ch.add("tensor", lambda e: e.matmul(
                            pb[1][:, :], sb_swap128[:, :], ROPB, start=True, stop=True))
                        def qcomb(e):
                            e.tensor_tensor(rows2q[:, 0:512], ROPB, COSB[:, 0:512],
                                            ALU.mult)
                            e.tensor_tensor(rows2q[:, 512:1024], pb[1][:, :],
                                            SINB[:, 0:512], ALU.mult)
                            e.tensor_tensor(QOUT, rows2q[:, 0:512],
                                            rows2q[:, 512:1024], ALU.add)
                            return e.tensor_tensor(rows2q[:, 0:512], QOUT, QOUT,
                                                   ALU.mult)
                        ch.add("vector", qcomb)
                        ch.add("tensor", lambda e: e.matmul(
                            pb[3][0:1, :], sb_ones[:], rows2q_bf_hack(e),
                            start=True, stop=True)) if False else None
                        ch.add("scalar", lambda e: e.copy(ROPB, rows2q[:, 0:512]))
                        ch.add("tensor", lambda e: e.matmul(
                            pb[3][0:1, :], sb_ones[:], ROPB, start=True, stop=True))
                        ch.add("vector", lambda e: e.tensor_reduce(
                            rows[:1, 1024:1025], pb[3][0:1, :],
                            mybir.AxisListType.X, ALU.max))
                        ch.add("vector", lambda e, qh=qh: e.tensor_tensor(
                            sb_mrow[:1, 32 + qh:33 + qh],
                            sb_mrow[:1, 32 + qh:33 + qh],
                            rows[:1, 1024:1025], ALU.max))
                        ch.add("sync", lambda e, qh=qh, tt=tt: e.dma_start(
                            out=qscr[qh][:, (tt - 1) * TT:tt * TT], in_=QOUT), kind="h")
            ch.add("scalar", lambda e: e.activation(rows[:1, 0:32],
                                                    sb_mrow[:1, 32:64], AF.Sqrt))
            ch.add("vector", lambda e: e.tensor_scalar(
                rows[:1, 0:16], rows[:1, 0:16],
                rows[:1, 16:17], -SC128, ALU.mult, ALU.mult))
            ch.add("vector", lambda e: e.tensor_copy(rowsb[:1, 0:16],
                                                     rows[:1, 0:16]))
            ch.add("tensor", lambda e: e.matmul(pb[0][:, 0:16], sb_onesr[:],
                                                rowsb[:1, 0:16], start=True, stop=True))
            ch.add("scalar", lambda e: e.copy(sb_mtil[:, 16:32], pb[0][:, 0:16]))
            for qh in range(GNQ):
                kvh = qh // 4
                for q2 in range(2):
                    ch.add("sync", lambda e, qh=qh, q2=q2: e.dma_start(
                        out=QOUT, in_=qscr[qh][:, q2 * TT:(q2 + 1) * TT]), kind="h")
                    def zmm(e):
                        e.matmul(pb[6][:, :], sb_zrow[:1, :], rowsb[:1, 0:512],
                                 start=True, stop=False)
                        return e.matmul(pb[7][0:1, :], sb_zrow[:1, 0:1],
                                        rowsb[:1, 0:512], start=True, stop=False)
                    ch.add("tensor", zmm)
                    for kb in range(2):
                        def sm(e, qh=qh, q2=q2, kb=kb, kvh=kvh):
                            mm = None
                            for sub in range(4):
                                mm = e.matmul(
                                    pb[sub][:, :],
                                    KST[:, kvh * 1536 + (q2 + kb) * 512 + sub * 128:
                                        kvh * 1536 + (q2 + kb) * 512 + sub * 128 + 128],
                                    QOUT, start=True, stop=True)
                            return mm
                        ch.add("tensor", sm)
                        def gexp(e, qh=qh):
                            last = None
                            for sub in range(4):
                                last = e.activation(
                                    PT[:, sub * 512:(sub + 1) * 512], pb[sub], AF.Exp,
                                    bias=sb_mtil[:, 16 + qh:17 + qh], scale=SC128)
                            return last
                        ch.add("scalar", gexp)
                        ch.add("gpsimd", lambda e, q2=q2, kb=kb: e.tensor_tensor(
                            PT[:], PT[:],
                            GMSK[:, (q2 * 2 + kb) * 2048:(q2 * 2 + kb + 1) * 2048],
                            ALU.mult))
                        def lav(e, q2=q2, kb=kb, kvh=kvh):
                            fin = (kb == 1)
                            mm = None
                            for sub in range(4):
                                e.matmul(pb[7][0:1, :], sb_ones[:],
                                         PT[:, sub * 512:(sub + 1) * 512],
                                         start=False, stop=(fin and sub == 3))
                                mm = e.matmul(
                                    pb[6][:, :],
                                    VST[:, ((q2 + kb) * 4 + sub) * 512 + kvh * 128:
                                        ((q2 + kb) * 4 + sub) * 512 + kvh * 128 + 128],
                                    PT[:, sub * 512:(sub + 1) * 512],
                                    start=False, stop=(fin and sub == 3))
                            return mm
                        ch.add("tensor", lav)
                    def rl(e):
                        e.reciprocal(rows[:1, 0:512], pb[7][0:1, :])
                        return e.tensor_copy(rowsb2[:1, 0:512], rows[:1, 0:512])
                    ch.add("vector", rl)
                    ch.add("tensor", lambda e: e.matmul(
                        pb[5][:, :], sb_onesr[:], rowsb2[:1, 0:512],
                        start=True, stop=True))
                    ch.add("scalar", lambda e: e.copy(RLB[:, 0:512], pb[5]))
                    ch.add("vector", lambda e, qh=qh, q2=q2: e.tensor_tensor(
                        ogt(q2, qh)[:], pb[6][:, :],
                        RLB[:, 0:512], ALU.mult))
            for dch in range(DCH):
                ch.add("gpsimd", lambda e, dch=dch: e.dma_start(
                    out=WTI0[:].rearrange("p (d m) -> p d m", d=16),
                    in_=G["Wo_b"].rearrange("(d p) m -> p d m", p=128)
                    [:, :, dch * 128:(dch + 1) * 128]), kind="s")
                def wmm(e, dch=dch):
                    mm = None
                    for q2 in range(2):
                        for hc in range(DCH):
                            mm = e.matmul(pb[q2][:, :],
                                          WTI0[:, hc * 128:(hc + 1) * 128],
                                          ogt(q2, hc)[:],
                                          start=(hc == 0), stop=(hc == DCH - 1))
                    return mm
                ch.add("tensor", wmm)
                def res(e, dch=dch):
                    last = None
                    for q2 in range(2):
                        xsl = sb_x[:, dch * EXT + (q2 + 1) * 512:
                                   dch * EXT + (q2 + 2) * 512]
                        last = e.scalar_tensor_tensor(xsl, pb[q2][:, :],
                                                      sb_bop[:, dch:dch + 1], xsl,
                                                      ALU.add, ALU.add)
                    return last
                ch.add("vector", res)
            ch.emit()

        def p12():
            def f(e):
                e.dma_start(
                    out=yout.rearrange("(c p) t -> p c t", p=128),
                    in_=sb_x[:].rearrange("p (c t) -> p c t", c=DCH)[:, :, HALO:EXT],
                ).then_inc(S["o0"], 16)
                e.wait_ge(S["o0"], 16)
            blk.sync(f)

        p0()
        barrier()
        if upto >= 1:
            p1()
            barrier()
        if upto >= 2:
            p2()
            barrier()
        if upto >= 3:
            p5()
            barrier()
            pmfin()
            barrier()
        if upto >= 4:
            p6()
            barrier()
        if upto >= 5:
            p7()
            barrier()
        if upto >= 6:
            p8()
            barrier()
        if upto >= 7:
            p9()
            barrier()
        if upto >= 8:
            p10()
            barrier()
        if upto >= 9:
            p11()
            barrier()
        p12()

    return nc, yout


def kernel(**inputs):
    nc, _ = build_program(BUILD_UPTO)
    nc.compile()
    in_maps = [make_inputs(c, inputs) for c in range(NCORE)]
    res = run_bass_kernel_spmd(nc, in_maps, core_ids=list(range(NCORE)))
    out = np.zeros((1, L, D), np.float32)
    for c in range(NCORE):
        out[0, SEG * c:SEG * (c + 1), :] = np.asarray(res.results[c]["yout"]).T
    return out
